# version 02
# baseline (speedup 1.0000x reference)
import numpy as np
import ml_dtypes
import concourse.bass as bass
import concourse.mybir as mybir
from concourse.bass_utils import run_bass_kernel_spmd

F32 = mybir.dt.float32
BF16 = mybir.dt.bfloat16
AF = mybir.ActivationFunctionType
ALU = mybir.AluOpType
AX = mybir.AxisListType

D = 1024
S = 4096
NB = 8
T = 512
DC = 8
FCN = 32
NE = 8
NH = 16
DH = 64
NSUB = 4
DFF = 4096
EPS = 1e-5
EXP_SHIFT = -20.0

ENGINES = ("tensor", "vector", "scalar", "gpsimd", "sync")
DEBUG_DUMPS = False


class Prog:

    def __init__(self):
        self.ops = []
        self.last_writer = {}
        self.readers = {}

    def add(self, engine, fn, reads=(), writes=(), dma=False):
        idx = len(self.ops)
        deps = set()
        for r in reads:
            w = self.last_writer.get(r)
            if w is not None:
                deps.add(w)
        for w_ in writes:
            w = self.last_writer.get(w_)
            if w is not None:
                deps.add(w)
            for rd in self.readers.get(w_, ()):
                deps.add(rd)
        deps.discard(idx)
        wres = writes[0] if dma else None
        if dma:
            assert len(writes) == 1, "dma ops must write exactly one resource"
        self.ops.append(dict(engine=engine, fn=fn, dma=dma, deps=deps,
                             need_inc=False, idx=idx, wres=wres))
        for w_ in writes:
            self.last_writer[w_] = idx
            self.readers[w_] = []
        for r in reads:
            self.readers.setdefault(r, []).append(idx)
        return idx

    def resolve(self):
        for op in self.ops:
            kept = []
            for d in sorted(op["deps"]):
                p = self.ops[d]
                if p["engine"] == "tensor" and op["engine"] == "tensor" \
                        and not p["dma"] and not op["dma"]:
                    continue
                kept.append(d)
                if not p["dma"]:
                    p["need_inc"] = True
            op["edges"] = kept
        cc = {e: 0 for e in ENGINES}
        rc = {}
        for op in self.ops:
            e = op["engine"]
            if op["dma"]:
                r = op["wres"]
                rc[r] = rc.get(r, 0) + 1
                op["post"] = ("r", r, rc[r] * 16)
            elif op["need_inc"]:
                cc[e] += 1
                op["post"] = ("c", e, cc[e])
            else:
                op["post"] = None
        self.dma_final = {r: v * 16 for r, v in rc.items()}
        self.sem_keys = [("c", e) for e in ENGINES if e != "sync"] +                         [("r", r) for r in rc]
        know = {e: {} for e in ENGINES}
        for op in self.ops:
            k = know[op["engine"]]
            waits = {}
            for d in op["edges"]:
                p = self.ops[d]
                assert p["post"] is not None, (op["idx"], d)
                kind, pe, val = p["post"]
                sem = (kind, pe)
                if k.get(sem, 0) >= val:
                    continue
                if waits.get(sem, 0) < val:
                    waits[sem] = val
            for sem, val in waits.items():
                k[sem] = val
            op["waits"] = sorted(waits.items())
            if op["engine"] == "tensor" and op["post"] is not None and op["post"][0] == "c":
                k[("c", "tensor")] = op["post"][2]

    def emit(self, engine_name, eng, sems):
        for op in self.ops:
            if op["engine"] != engine_name:
                continue
            for (sem, val) in op["waits"]:
                eng.wait_ge(sems[sem], val)
            inst = op["fn"](eng)
            if op["dma"]:
                inst.then_inc(sems[("r", op["wres"])], 16)
            elif op["need_inc"]:
                if inst is None:
                    eng.nop().then_inc(sems[("c", engine_name)], 1)
                else:
                    inst.then_inc(sems[("c", engine_name)], 1)


def build_nc():
    nc = bass.Bass("TRN2", target_bir_lowering=False, debug=False)

    xin = nc.dram_tensor("x_fm", [DC, 128, NB, T], F32, kind="ExternalInput").ap()
    qkvw = nc.dram_tensor("qkvw", [DC, 128, 3 * D], F32, kind="ExternalInput").ap()
    rwT = nc.dram_tensor("rwT", [128, DC * NE], F32, kind="ExternalInput").ap()
    rbbc_in = nc.dram_tensor("rb_bc", [128, NE], F32, kind="ExternalInput").ap()
    b1T_in = nc.dram_tensor("b1T", [128, NE * FCN], F32, kind="ExternalInput").ap()
    b2T_in = nc.dram_tensor("b2T", [128, NE * DC], F32, kind="ExternalInput").ap()
    w1in = nc.dram_tensor("w1", [NE, DC, 128, DFF], F32, kind="ExternalInput").ap()
    w2in = nc.dram_tensor("w2", [NE, FCN, 128, D], F32, kind="ExternalInput").ap()
    outap = nc.dram_tensor("out_fm", [DC, 128, NB, T], F32, kind="ExternalOutput").ap()
    dbg = {}
    if DEBUG_DUMPS:
        for nm, shape, dt in [("dbg_normed1", [128, DC, T], BF16),
                              ("dbg_x2", [128, DC, T], F32),
                              ("dbg_gbc", [128, NE, T], BF16),
                              ("dbg_q", [128, DC, T], BF16),
                              ("dbg_k", [128, DC, T], BF16),
                              ("dbg_v", [128, NSUB, NH * 65], BF16),
                              ("dbg_x3", [128, DC, T], F32),
                              ("dbg_h", [128, FCN * T], BF16)]:
            dbg[nm] = nc.dram_tensor(nm, shape, dt, kind="ExternalOutput").ap()
        dbg["dbg_kdr"] = nc.dram_tensor("dbg_kdr", [DC, 128, S], BF16,
                                        kind="ExternalOutput").ap()
        dbg["dbg_kscr"] = nc.dram_tensor("dbg_kscr", [2, 4, 128, 128], BF16,
                                         kind="ExternalOutput").ap()

    w1b = [nc.dram_tensor(f"w1b{e}", [FCN, 128, DC, 128], BF16, kind="Internal").ap()
           for e in range(NE)]
    w2b = [nc.dram_tensor(f"w2b{e}", [FCN, 128, D], BF16, kind="Internal").ap()
           for e in range(NE)]
    qkvb = nc.dram_tensor("qkvb", [DC, 128, 3 * D], BF16, kind="Internal").ap()
    gsc = nc.dram_tensor("gsc", [1, NE, T], BF16, kind="Internal").ap()
    kdr = nc.dram_tensor("kdr", [DC, 128, S], BF16, kind="Internal").ap()
    vdr = nc.dram_tensor("vdr", [S, NH * 65], BF16, kind="Internal").ap()
    x2d = nc.dram_tensor("x2d", [DC, 128, NB, T], F32, kind="Internal").ap()
    qdr = nc.dram_tensor("qdr", [DC, 128, NB, T], BF16, kind="Internal").ap()
    kscr = nc.dram_tensor("kscr", [NH // 2, FCN, 128, 128], BF16, kind="Internal").ap()

    sb = {}

    def sbuf(name, shape, dt):
        sb[name] = nc.alloc_sbuf_tensor(name, shape, dt)
        return sb[name]

    rwt = sbuf("rwt", [128, DC * NE], BF16)
    rbbc = sbuf("rbbc", [128, NE], F32)
    b1t = sbuf("b1t", [128, NE * FCN], F32)
    b2t = sbuf("b2t", [128, NE * DC], F32)
    iden = sbuf("iden", [128, 128], F32)
    iotap = sbuf("iotap", [128, 1], F32)
    iota8 = sbuf("iota8", [128, 8], F32)
    iom9 = sbuf("iom9", [128, 8], F32)
    onesc = sbuf("onesc", [128, 1], F32)
    ones1 = sbuf("ones1", [1, 128], F32)
    masks = sbuf("masks", [128, 4, T], BF16)
    iotaq = sbuf("iotaq", [128, T], F32)

    xt = sbuf("xt", [128, 2, DC, T], F32)
    acc = sbuf("acc", [128, DC, T], F32)
    normed = sbuf("normed", [128, DC, T], BF16)
    hbuf = sbuf("hbuf", [128, FCN * T], BF16)
    sqv = sb["hbuf"].bitcast(F32)
    gbc = sbuf("gbc", [128, NE, T], BF16)
    qbf = sbuf("qbf", [128, DC, T], BF16)
    kst = sbuf("kst", [128, DC, T], BF16)
    vst = sbuf("vst", [128, NSUB, NH * 65], BF16)

    w1r = [sbuf(f"w1r{s}", [128, DC, 128], BF16) for s in range(4)]
    w2r = [sbuf(f"w2r{s}", [128, DC, 128], BF16) for s in range(4)]
    qkr = [sbuf(f"qkr{s}", [128, 1024], BF16) for s in range(2)]
    ksrng = [sbuf(f"ksrng{s}", [128, 128], BF16) for s in range(6)]
    katile = [sbuf(f"katile{s}", [128, 128], BF16) for s in range(2)]
    kstg = [sbuf(f"kstg{s}", [128, 128], BF16) for s in range(2)]
    iden_bf = sbuf("iden_bf", [128, 128], BF16)
    vrng = [sbuf(f"vrng{s}", [128, NH * 65], BF16) for s in range(4)]
    erng = [sbuf(f"erng{s}", [128, T], BF16) for s in range(4)]
    cstg = [sbuf(f"cstg{s}", [128, 1024], BF16) for s in range(2)]

    rows = sbuf("rows", [1, 7 * T], F32)
    grow8 = sbuf("grow8", [1, NE * T], BF16)
    eps1 = sbuf("eps1", [1, 1], F32)
    shf = sbuf("shf", [128, 1], F32)
    ones1b = sbuf("ones1b", [1, 128], BF16)
    a_sb = sbuf("a_sb", [128, T], F32)
    b_sb = sbuf("b_sb", [128, T], F32)
    rsbc = sbuf("rsbc", [128, T], F32)
    tmp1 = sbuf("tmp1", [128, T], F32)
    tmp2 = sbuf("tmp2", [128, T], F32)
    lgt = sbuf("lgt", [128, 8], F32)
    lmax = sbuf("lmax", [128, 1], F32)
    negm = sbuf("negm", [128, 1], F32)
    expv = sbuf("expv", [128, 8], F32)
    sume = sbuf("sume", [128, 1], F32)
    gval = sbuf("gval", [128, 1], F32)
    meq = sbuf("meq", [128, 8], F32)
    selv = sbuf("selv", [128, 8], F32)
    emin = sbuf("emin", [128, 1], F32)
    g8 = sbuf("g8", [128, 8], F32)
    grs = sbuf("grs", [8, T], F32)

    sbuf_left = nc.sbuf_bytes_remaining
    assert sbuf_left >= 0, sbuf_left

    R_MU, R_MSQ, R_VAR, R_LNV, R_A, R_B, R_RS = range(7)

    def row(i):
        return sb["rows"][0:1, i * T:(i + 1) * T]

    P = Prog()

    with (
        nc.psum_tensor("pb0", [128, T], F32) as pb0,
        nc.psum_tensor("pb1", [128, T], F32) as pb1,
        nc.psum_tensor("pb2", [128, T], F32) as pb2,
        nc.psum_tensor("pb3", [128, T], F32) as pb3,
        nc.psum_tensor("pb4", [128, T], F32) as pb4,
        nc.psum_tensor("pb5", [128, T], F32) as pb5,
        nc.psum_tensor("pb6", [128, T], F32) as pb6,
        nc.psum_tensor("pb7", [128, T], F32) as pb7,
    ):
        pb = [pb0, pb1, pb2, pb3, pb4, pb5, pb6, pb7]
        pname = [f"pb{i}" for i in range(8)]

        def prologue():
            P.add("gpsimd", lambda e: e.iota(iotap[:], [[0, 1]], channel_multiplier=1,
                                             allow_small_or_imprecise_dtypes=True),
                  writes=["iotap"])
            P.add("gpsimd", lambda e: e.iota(iota8[:], [[1, 8]], channel_multiplier=0,
                                             allow_small_or_imprecise_dtypes=True),
                  writes=["iota8"])
            P.add("gpsimd", lambda e: e.iota(iotaq[:], [[1, T]], channel_multiplier=0,
                                             allow_small_or_imprecise_dtypes=True),
                  writes=["iotaq"])
            P.add("vector", lambda e: e.tensor_scalar(iom9[:], iota8[:], -9.0, None,
                                                      ALU.add),
                  reads=["iota8"], writes=["iom9"])
            P.add("vector", lambda e: e.memset(onesc[:], 1.0), writes=["onesc"])
            P.add("vector", lambda e: e.memset(ones1[:], 1.0), writes=["ones1"])
            P.add("vector", lambda e: e.memset(ones1b[:], 1.0), writes=["ones1b"])
            P.add("vector", lambda e: e.memset(eps1[:], EPS), writes=["eps1"])
            P.add("vector", lambda e: e.memset(shf[:], EXP_SHIFT), writes=["shf"])
            P.add("vector", lambda e: e.memset(vst[:], 1.0), writes=["vst"])
            P.add("gpsimd", lambda e: e.iota(iden[:], [[1, 128]], channel_multiplier=0,
                                             allow_small_or_imprecise_dtypes=True),
                  writes=["iden"])
            P.add("vector", lambda e: e.tensor_scalar(iden[:], iden[:], iotap[:], None,
                                                      ALU.is_equal),
                  reads=["iotap"], writes=["iden"])
            P.add("vector", lambda e: e.tensor_copy(iden_bf[:], iden[:]),
                  reads=["iden"], writes=["iden_bf"])
            for m in range(4):
                def mk(e, m=m):
                    return e.tensor_scalar(negm[:], iotap[:], float(128 * m), None, ALU.add)
                P.add("vector", mk, reads=["iotap"], writes=["negm"])
                def mk2(e, m=m):
                    return e.tensor_scalar(masks[:, m, :], iotaq[:], negm[:], None, ALU.is_ge)
                P.add("vector", mk2, reads=["iotaq", "negm"], writes=["masks"])
            P.add("gpsimd", lambda e: e.dma_start(out=rwt[:], in_=rwT), dma=True,
                  writes=["rwt"])
            P.add("sync", lambda e: e.dma_start(out=rbbc[:], in_=rbbc_in), dma=True,
                  writes=["rbbc"])
            P.add("sync", lambda e: e.dma_start(out=b1t[:], in_=b1T_in), dma=True,
                  writes=["b1t"])
            P.add("sync", lambda e: e.dma_start(out=b2t[:], in_=b2T_in), dma=True,
                  writes=["b2t"])
            slc = [0]
            for e_ in range(NE):
                for fs in range(FCN):
                    sl = slc[0] % 2
                    slc[0] += 1
                    def ld(eng, e_=e_, fs=fs, sl=sl):
                        src = w1in[e_, :, :, fs * 128:(fs + 1) * 128].transpose([1, 0, 2])
                        return eng.dma_start(out=cstg[sl][:].rearrange(
                            "p (dc f) -> p dc f", dc=DC), in_=src)
                    P.add("gpsimd", ld, dma=True, writes=[f"cstg{sl}"])
                    def st(eng, e_=e_, fs=fs, sl=sl):
                        return eng.dma_start(out=w1b[e_][fs], in_=cstg[sl][:].rearrange(
                            "p (dc f) -> p dc f", dc=DC))
                    P.add("gpsimd", st, dma=True, reads=[f"cstg{sl}"], writes=[f"w1bd{e_}"])
                for fc in range(FCN):
                    sl = slc[0] % 2
                    slc[0] += 1
                    def ld2(eng, e_=e_, fc=fc, sl=sl):
                        return eng.dma_start(out=cstg[sl][:], in_=w2in[e_, fc])
                    P.add("gpsimd", ld2, dma=True, writes=[f"cstg{sl}"])
                    def st2(eng, e_=e_, fc=fc, sl=sl):
                        return eng.dma_start(out=w2b[e_][fc], in_=cstg[sl][:])
                    P.add("gpsimd", st2, dma=True, reads=[f"cstg{sl}"], writes=[f"w2bd{e_}"])
            for dc in range(DC):
                for qs in range(3):
                    sl = slc[0] % 2
                    slc[0] += 1
                    def ld3(eng, dc=dc, qs=qs, sl=sl):
                        return eng.dma_start(out=cstg[sl][:],
                                             in_=qkvw[dc][:, qs * 1024:(qs + 1) * 1024])
                    P.add("gpsimd", ld3, dma=True, writes=[f"cstg{sl}"])
                    def st3(eng, dc=dc, qs=qs, sl=sl):
                        return eng.dma_start(out=qkvb[dc][:, qs * 1024:(qs + 1) * 1024],
                                             in_=cstg[sl][:])
                    P.add("gpsimd", st3, dma=True, reads=[f"cstg{sl}"], writes=["qkvbd"])

        def layernorm(src_view, src_res):
            for dc in range(DC):
                def sq(e, dc=dc):
                    return e.activation(sqv[:, dc * T:(dc + 1) * T], src_view(dc), AF.Square)
                P.add("scalar", sq, reads=[src_res], writes=["hbuf"])
            for dc in range(DC):
                def mm(e, dc=dc):
                    return e.matmul(pb6[0:1, :], onesc[:], src_view(dc),
                                    start=(dc == 0), stop=(dc == DC - 1))
                P.add("tensor", mm, reads=[src_res, "onesc"], writes=["pb6"])
            for dc in range(DC):
                def mm2(e, dc=dc):
                    return e.matmul(pb7[0:1, :], onesc[:], sqv[:, dc * T:(dc + 1) * T],
                                    start=(dc == 0), stop=(dc == DC - 1))
                P.add("tensor", mm2, reads=["hbuf", "onesc"], writes=["pb7"])
            P.add("vector", lambda e: e.tensor_scalar(row(R_MU), pb6[0:1, :], 1.0 / D, None,
                                                      ALU.mult),
                  reads=["pb6"], writes=["rows"])
            P.add("vector", lambda e: e.tensor_scalar(row(R_MSQ), pb7[0:1, :], 1.0 / D, None,
                                                      ALU.mult),
                  reads=["pb7"], writes=["rows"])
            P.add("vector", lambda e: e.scalar_tensor_tensor(row(R_VAR), row(R_MU), -1.0,
                                                             row(R_MU), ALU.mult, ALU.mult),
                  reads=["rows"], writes=["rows"])
            P.add("vector", lambda e: e.tensor_tensor(row(R_VAR), row(R_VAR), row(R_MSQ),
                                                      ALU.add),
                  reads=["rows"], writes=["rows"])
            P.add("scalar", lambda e: e.activation(row(R_LNV), row(R_VAR), AF.Ln,
                                                   bias=eps1[:]),
                  reads=["rows", "eps1"], writes=["rows"])
            P.add("scalar", lambda e: e.activation(row(R_A), row(R_LNV), AF.Exp, scale=-0.5),
                  reads=["rows"], writes=["rows"])
            P.add("vector", lambda e: e.scalar_tensor_tensor(row(R_B), row(R_MU), -1.0,
                                                             row(R_A), ALU.mult, ALU.mult),
                  reads=["rows"], writes=["rows"])
            P.add("tensor", lambda e: e.matmul(pb6[:], ones1[:], row(R_A), start=True,
                                               stop=True),
                  reads=["rows", "ones1"], writes=["pb6"])
            P.add("tensor", lambda e: e.matmul(pb7[:], ones1[:], row(R_B), start=True,
                                               stop=True),
                  reads=["rows", "ones1"], writes=["pb7"])
            P.add("vector", lambda e: e.tensor_copy(a_sb[:], pb6[:]),
                  reads=["pb6"], writes=["a_sb"])
            P.add("vector", lambda e: e.tensor_copy(b_sb[:], pb7[:]),
                  reads=["pb7"], writes=["b_sb"])
            for dc in range(DC):
                def n1(e, dc=dc):
                    return e.tensor_tensor(tmp1[:], src_view(dc), a_sb[:], ALU.mult)
                P.add("vector", n1, reads=[src_res, "a_sb"], writes=["tmp1"])
                def n2(e, dc=dc):
                    return e.tensor_tensor(normed[:, dc, :], tmp1[:], b_sb[:], ALU.add)
                P.add("vector", n2, reads=["tmp1", "b_sb"], writes=["normed"])

        def router():
            tbanks = [2, 3, 4, 5]
            for sub in range(NSUB):
                for dc in range(DC):
                    def mm(e, sub=sub, dc=dc):
                        return e.matmul(pb6[:, 0:8],
                                        normed[:, dc, sub * 128:(sub + 1) * 128],
                                        rwt[:, dc * 8:(dc + 1) * 8],
                                        start=(dc == 0), stop=(dc == DC - 1))
                    P.add("tensor", mm, reads=["normed", "rwt"], writes=["pb6"])
                P.add("vector", lambda e: e.tensor_tensor(lgt[:], pb6[:, 0:8], rbbc[:],
                                                          ALU.add),
                      reads=["pb6", "rbbc"], writes=["lgt"])
                P.add("vector", lambda e: e.tensor_reduce(lmax[:], lgt[:], AX.X, ALU.max),
                      reads=["lgt"], writes=["lmax"])
                P.add("vector", lambda e: e.tensor_scalar(negm[:], lmax[:], -1.0, None,
                                                          ALU.mult),
                      reads=["lmax"], writes=["negm"])
                P.add("scalar", lambda e: e.activation(expv[:], lgt[:], AF.Exp,
                                                       bias=negm[:]),
                      reads=["lgt", "negm"], writes=["expv"])
                P.add("vector", lambda e: e.tensor_reduce(sume[:], expv[:], AX.X, ALU.add),
                      reads=["expv"], writes=["sume"])
                P.add("vector", lambda e: e.reciprocal(gval[:], sume[:]),
                      reads=["sume"], writes=["gval"])
                P.add("vector", lambda e: e.tensor_scalar(gval[:], gval[:], 0.5, None,
                                                          ALU.mult),
                      reads=["gval"], writes=["gval"])
                P.add("vector", lambda e: e.tensor_scalar(meq[:], lgt[:], lmax[:], None,
                                                          ALU.is_equal),
                      reads=["lgt", "lmax"], writes=["meq"])
                P.add("vector", lambda e: e.tensor_tensor(selv[:], meq[:], iom9[:],
                                                          ALU.mult),
                      reads=["meq", "iom9"], writes=["selv"])
                P.add("vector", lambda e: e.tensor_scalar(selv[:], selv[:], 9.0, None,
                                                          ALU.add),
                      reads=["selv"], writes=["selv"])
                P.add("vector", lambda e: e.tensor_reduce(emin[:], selv[:], AX.X, ALU.min),
                      reads=["selv"], writes=["emin"])
                P.add("vector", lambda e: e.tensor_scalar(meq[:], iota8[:], emin[:], None,
                                                          ALU.is_equal),
                      reads=["iota8", "emin"], writes=["meq"])
                P.add("vector", lambda e: e.tensor_scalar(g8[:], meq[:], gval[:], None,
                                                          ALU.mult),
                      reads=["meq", "gval"], writes=["g8"])
                bk = tbanks[sub]
                def tr(e, sub=sub, bk=bk):
                    return e.transpose(pb[bk][0:8, 0:128], g8[:], iden[:])
                P.add("tensor", tr, reads=["g8", "iden"], writes=[pname[bk]])
                def ev(e, sub=sub, bk=bk):
                    return e.tensor_copy(grs[:, sub * 128:(sub + 1) * 128],
                                         pb[bk][0:8, 0:128])
                P.add("vector", ev, reads=[pname[bk]], writes=["grs"])
            P.add("gpsimd", lambda e: e.dma_start(out=gsc[0], in_=grs[:]), dma=True,
                  reads=["grs"], writes=["gscd"])
            def mvrow(e):
                dst = grow8[0:1, :].rearrange("o (e t) -> o e t", e=8)
                return e.dma_start(out=dst, in_=gsc)
            P.add("sync", mvrow, dma=True, reads=["gscd"], writes=["grow8"])
            for e_ in range(NE):
                bk = 6 if e_ % 2 == 0 else 7
                def mm(e, e_=e_, bk=bk):
                    return e.matmul(pb[bk][:], ones1b[:],
                                    grow8[0:1, e_ * T:(e_ + 1) * T], start=True, stop=True)
                P.add("tensor", mm, reads=["grow8", "ones1b"], writes=[pname[bk]])
                def ev2(e, e_=e_, bk=bk):
                    return e.tensor_copy(gbc[:, e_, :], pb[bk][:])
                P.add("vector", ev2, reads=[pname[bk]], writes=["gbc"])

        def ffn(iblk):
            def ld_w1(e_, fc):
                sl = fc % 4
                def ld(eng, e_=e_, fc=fc, sl=sl):
                    return eng.dma_start(out=w1r[sl][:], in_=w1b[e_][fc])
                P.add("sync", ld, dma=True, reads=[f"w1bd{e_}"], writes=[f"w1r{sl}"])

            def ld_w2(e_, fc):
                sl = fc % 4
                def ld2(eng, e_=e_, fc=fc, sl=sl):
                    return eng.dma_start(out=w2r[sl][:].rearrange(
                        "p dc f -> p (dc f)"), in_=w2b[e_][fc])
                P.add("sync", ld2, dma=True, reads=[f"w2bd{e_}"], writes=[f"w2r{sl}"])

            for e_ in range(NE):
                for fc in range(4):
                    ld_w1(e_, fc)
                for fc in range(FCN):
                    sl = fc % 4
                    hb = 6 if fc % 2 == 0 else 7
                    for dc in range(DC):
                        def mm(e, fc=fc, dc=dc, sl=sl, hb=hb):
                            return e.matmul(pb[hb][:], w1r[sl][:, dc, :], normed[:, dc, :],
                                            start=(dc == 0), stop=(dc == DC - 1))
                        P.add("tensor", mm, reads=[f"w1r{sl}", "normed"], writes=[pname[hb]])
                    if fc + 4 < FCN:
                        ld_w1(e_, fc + 4)
                    def relu(e, e_=e_, fc=fc, hb=hb):
                        col = e_ * FCN + fc
                        return e.activation(hbuf[:, fc * T:(fc + 1) * T], pb[hb][:],
                                            AF.Relu, bias=b1t[:, col:col + 1])
                    P.add("scalar", relu, reads=[pname[hb], "b1t"], writes=["hbuf"])
                for fc in range(4):
                    ld_w2(e_, fc)
                for half in range(2):
                    for fc in range(FCN):
                        sl = fc % 4
                        for dci in range(4):
                            dc = half * 4 + dci
                            yb = 2 + dci
                            def mm2(e, fc=fc, dc=dc, sl=sl, yb=yb):
                                return e.matmul(pb[yb][:], w2r[sl][:, dc, :],
                                                hbuf[:, fc * T:(fc + 1) * T],
                                                start=(fc == 0), stop=(fc == FCN - 1))
                            P.add("tensor", mm2, reads=[f"w2r{sl}", "hbuf"],
                                  writes=[pname[yb]])
                        if fc + 4 < FCN:
                            ld_w2(e_, fc + 4)
                    if half == 0:
                        for fc in range(4):
                            ld_w2(e_, fc)
                    for dci in range(4):
                        dc = half * 4 + dci
                        yb = 2 + dci
                        def ev(e, e_=e_, dc=dc, yb=yb):
                            col = e_ * DC + dc
                            return e.scalar_tensor_tensor(tmp2[:], pb[yb][:],
                                                          b2t[:, col:col + 1],
                                                          gbc[:, e_, :], ALU.add, ALU.mult)
                        P.add("vector", ev, reads=[pname[yb], "b2t", "gbc"], writes=["tmp2"])
                        def ac(e, dc=dc):
                            return e.tensor_tensor(acc[:, dc, :], acc[:, dc, :], tmp2[:],
                                                   ALU.add)
                        P.add("vector", ac, reads=["tmp2", "acc"], writes=["acc"])

        def qkv(iblk):
            ocgroups = [list(range(0, 6)), list(range(6, 12)), list(range(12, 16))]
            for grp in ocgroups:
                base = grp[0] * 128
                width = len(grp) * 128
                for dc in range(DC):
                    sl = dc % 2
                    def ld(eng, dc=dc, sl=sl, base=base, width=width):
                        return eng.dma_start(out=qkr[sl][:, 0:width],
                                             in_=qkvb[dc][:, base:base + width])
                    P.add("sync", ld, dma=True, reads=["qkvbd"], writes=[f"qkr{sl}"])
                    for k, oc in enumerate(grp):
                        def mm(e, dc=dc, sl=sl, k=k):
                            return e.matmul(pb[k][:], qkr[sl][:, k * 128:(k + 1) * 128],
                                            normed[:, dc, :], start=(dc == 0),
                                            stop=(dc == DC - 1))
                        P.add("tensor", mm, reads=[f"qkr{sl}", "normed"], writes=[pname[k]])
                for k, oc in enumerate(grp):
                    if oc < 8:
                        def ev(e, k=k, oc=oc):
                            return e.tensor_copy(qbf[:, oc, :], pb[k][:])
                        P.add("vector", ev, reads=[pname[k]], writes=["qbf"])
                    else:
                        def ev(e, k=k, oc=oc):
                            return e.tensor_copy(kst[:, oc - 8, :], pb[k][:])
                        P.add("vector", ev, reads=[pname[k]], writes=["kst"])
            for grp in ([(0, 0), (0, 1), (1, 0), (1, 1)], [(2, 0), (2, 1), (3, 0), (3, 1)]):
                for dc in range(DC):
                    sl = dc % 2
                    def ld2(eng, dc=dc, sl=sl):
                        return eng.dma_start(out=qkr[sl][:], in_=qkvb[dc][:, 2 * D:3 * D])
                    P.add("sync", ld2, dma=True, reads=["qkvbd"], writes=[f"qkr{sl}"])
                    for k, (sub, vh) in enumerate(grp):
                        def mm(e, dc=dc, sl=sl, sub=sub, vh=vh, k=k):
                            return e.matmul(pb[2 + k][:],
                                            normed[:, dc, sub * 128:(sub + 1) * 128],
                                            qkr[sl][:, vh * 512:(vh + 1) * 512],
                                            start=(dc == 0), stop=(dc == DC - 1))
                        P.add("tensor", mm, reads=[f"qkr{sl}", "normed"],
                              writes=[pname[2 + k]])
                for k, (sub, vh) in enumerate(grp):
                    def ev2(e, k=k, sub=sub, vh=vh):
                        dst = vst[:, sub, :].rearrange("p (h c) -> p h c", c=65)
                        dst = dst[:, 8 * vh:8 * vh + 8, 0:64]
                        src = pb[2 + k][:].rearrange("p (h c) -> p h c", c=64)
                        return e.tensor_copy(dst, src)
                    P.add("vector", ev2, reads=[pname[2 + k]], writes=["vst"])
            def stk(e, iblk=iblk):
                dst = kdr[:, :, iblk * T:(iblk + 1) * T].transpose([1, 0, 2])
                return e.dma_start(out=dst, in_=kst[:].rearrange("p dc t -> p dc t"))
            P.add("gpsimd", stk, dma=True, reads=["kst"], writes=["kdrd"])
            def stv(e, iblk=iblk):
                dst = vdr[iblk * T:(iblk + 1) * T, :].rearrange("(sub p) c -> p sub c", p=128)
                return e.dma_start(out=dst, in_=vst[:])
            P.add("gpsimd", stv, dma=True, reads=["vst"], writes=["vdrd"])

        def kscr_pass():
            cnt = [0]
            for hp in range(NH // 2):
                for jc in range(FCN):
                    sl = cnt[0] % 2
                    hb = 6 if cnt[0] % 2 == 0 else 7
                    cnt[0] += 1
                    def ld(e, hp=hp, jc=jc, sl=sl):
                        return e.dma_start(
                            out=katile[sl][:],
                            in_=kdr[hp, :, :].rearrange(
                                "(hh pr) (m c) -> hh pr m c", hh=2, c=64)[
                                :, 2 * jc:2 * jc + 2, :, :].transpose([1, 2, 0, 3]))
                    P.add("scalar", ld, dma=True, reads=["kdrd"], writes=[f"katile{sl}"])
                    def tr(e, sl=sl, hb=hb):
                        out_bf = pb[hb].bitcast(BF16)[0:128, 0:128]
                        return e.transpose(out_bf, katile[sl][:], iden_bf[:])
                    P.add("tensor", tr, reads=[f"katile{sl}", "iden_bf"], writes=[pname[hb]])
                    def ev(e, sl=sl, hb=hb):
                        return e.tensor_copy(kstg[sl][:], pb[hb].bitcast(BF16)[0:128, 0:128])
                    P.add("vector", ev, reads=[pname[hb]], writes=[f"kstg{sl}"])
                    def st(e, hp=hp, jc=jc, sl=sl):
                        return e.dma_start(out=kscr[hp, jc], in_=kstg[sl][:])
                    P.add("gpsimd", st, dma=True, reads=[f"kstg{sl}"], writes=["kscrd"])

        def attention(iblk):
            jlast = 4 * iblk + 3
            passes = [list(range(0, 6)), list(range(6, 12)), list(range(12, 16))]
            for heads in passes:
                npair = len(heads) // 2
                base = heads[0]
                kcnt = [0]
                kslot = {}
                def ldkv(j, pairs_hp):
                    sl = j % 4
                    def f2(e, j=j, sl=sl):
                        return e.dma_start(out=vrng[sl][:], in_=vdr[j * 128:(j + 1) * 128, :])
                    P.add("scalar", f2, dma=True, reads=["vdrd"], writes=[f"vrng{sl}"])
                    for hp in pairs_hp:
                        ks = kcnt[0] % 6
                        kcnt[0] += 1
                        kslot[(j, hp)] = ks
                        def f3(e, j=j, hp=hp, ks=ks):
                            return e.dma_start(out=ksrng[ks][:], in_=kscr[hp, j])
                        P.add("scalar", f3, dma=True, reads=["kscrd"],
                              writes=[f"ksrng{ks}"])
                pairs_hp = [(base // 2) + pi for pi in range(npair)]
                ldkv(0, pairs_hp)
                esl = 0
                for j in range(jlast + 1):
                    if j + 1 <= jlast:
                        ldkv(j + 1, pairs_hp)
                    sl = j % 4
                    m = j - 4 * iblk
                    for pi in range(npair):
                        c = (base // 2) + pi
                        hA, hB = 2 * c, 2 * c + 1
                        eA, eB = esl % 4, (esl + 1) % 4
                        esl += 2
                        ks = kslot[(j, c)]
                        def mmA(e, c=c, ks=ks):
                            return e.matmul(pb6[:], ksrng[ks][0:64, :],
                                            qbf[0:64, c, :], start=True, stop=True)
                        P.add("tensor", mmA, reads=[f"ksrng{ks}", "qbf"], writes=["pb6"])
                        def mmB(e, c=c, ks=ks):
                            return e.matmul(pb7[:], ksrng[ks][64:128, :],
                                            qbf[64:128, c, :], start=True, stop=True)
                        P.add("tensor", mmB, reads=[f"ksrng{ks}", "qbf"], writes=["pb7"])
                        def exA(e, eA=eA):
                            return e.activation(erng[eA][:], pb6[:], AF.Exp, scale=0.125,
                                                bias=shf[:])
                        P.add("scalar", exA, reads=["pb6", "shf"], writes=[f"erng{eA}"])
                        def exB(e, eB=eB):
                            return e.activation(erng[eB][:], pb7[:], AF.Exp, scale=0.125,
                                                bias=shf[:])
                        P.add("scalar", exB, reads=["pb7", "shf"], writes=[f"erng{eB}"])
                        if 0 <= m <= 3:
                            def mkA(e, eA=eA, m=m):
                                return e.tensor_tensor(erng[eA][:], erng[eA][:],
                                                       masks[:, m, :], ALU.mult)
                            P.add("vector", mkA, reads=[f"erng{eA}", "masks"],
                                  writes=[f"erng{eA}"])
                            def mkB(e, eB=eB, m=m):
                                return e.tensor_tensor(erng[eB][:], erng[eB][:],
                                                       masks[:, m, :], ALU.mult)
                            P.add("vector", mkB, reads=[f"erng{eB}", "masks"],
                                  writes=[f"erng{eB}"])
                        bA = hA - base
                        bB = hB - base
                        def pvA(e, hA=hA, sl=sl, eA=eA, bA=bA, j=j):
                            return e.matmul(pb[bA][0:65, :],
                                            vrng[sl][:, 65 * hA:65 * hA + 65],
                                            erng[eA][:], start=(j == 0), stop=(j == jlast))
                        P.add("tensor", pvA, reads=[f"vrng{sl}", f"erng{eA}"],
                              writes=[pname[bA]])
                        def pvB(e, hB=hB, sl=sl, eB=eB, bB=bB, j=j):
                            return e.matmul(pb[bB][0:65, :],
                                            vrng[sl][:, 65 * hB:65 * hB + 65],
                                            erng[eB][:], start=(j == 0), stop=(j == jlast))
                        P.add("tensor", pvB, reads=[f"vrng{sl}", f"erng{eB}"],
                              writes=[pname[bB]])
                for h in heads:
                    bk = h - base
                    def rc(e, bk=bk):
                        return e.reciprocal(row(R_RS), pb[bk][64:65, :])
                    P.add("vector", rc, reads=[pname[bk]], writes=["rows"])
                    P.add("tensor", lambda e: e.matmul(pb6[:], ones1[:], row(R_RS),
                                                       start=True, stop=True),
                          reads=["rows", "ones1"], writes=["pb6"])
                    P.add("vector", lambda e: e.tensor_copy(rsbc[:], pb6[:]),
                          reads=["pb6"], writes=["rsbc"])
                    po = 64 * (h % 2)
                    dc = h // 2
                    def fin1(e, bk=bk, po=po):
                        return e.tensor_tensor(tmp1[po:po + 64, :], pb[bk][0:64, :],
                                               rsbc[po:po + 64, :], ALU.mult)
                    P.add("vector", fin1, reads=[pname[bk], "rsbc"], writes=["tmp1"])
                    def fin2(e, dc=dc, po=po):
                        return e.tensor_tensor(acc[po:po + 64, dc, :],
                                               acc[po:po + 64, dc, :],
                                               tmp1[po:po + 64, :], ALU.add)
                    P.add("vector", fin2, reads=["tmp1", "acc"], writes=["acc"])

        def dump(i, name, tile, res):
            if DEBUG_DUMPS and i == 0:
                P.add("gpsimd", lambda e: e.dma_start(out=dbg[name], in_=tile[:]),
                      dma=True, reads=[res], writes=[name])

        def build_block_a(i):
            s = i % 2
            if i == 0:
                def ldx(e):
                    return e.dma_start(out=xt[:, 0, :, :], in_=xin[:, :, 0, :].transpose(
                        [1, 0, 2]))
                P.add("sync", ldx, dma=True, writes=["xt0"])
            if i + 1 < NB:
                def ldx2(e, i=i):
                    return e.dma_start(out=xt[:, (i + 1) % 2, :, :],
                                       in_=xin[:, :, i + 1, :].transpose([1, 0, 2]))
                P.add("sync", ldx2, dma=True, writes=[f"xt{(i + 1) % 2}"])

            xres = f"xt{s}"
            xv = lambda dc, s=s: xt[:, s, dc, :]
            layernorm(xv, xres)
            dump(i, "dbg_normed1", normed, "normed")
            router()
            dump(i, "dbg_gbc", gbc, "gbc")
            for dc in range(DC):
                def cp(e, dc=dc, s=s):
                    return e.tensor_copy(acc[:, dc, :], xt[:, s, dc, :])
                P.add("vector", cp, reads=[xres], writes=["acc"])
            ffn(i)
            dump(i, "dbg_h", sb["hbuf"], "hbuf")
            dump(i, "dbg_x2", acc, "acc")
            av = lambda dc: acc[:, dc, :]
            layernorm(av, "acc")
            def stx2(e, i=i):
                return e.dma_start(out=x2d[:, :, i, :].transpose([1, 0, 2]), in_=acc[:])
            P.add("gpsimd", stx2, dma=True, reads=["acc"], writes=["x2dd"])
            qkv(i)
            def stq(e, i=i):
                return e.dma_start(out=qdr[:, :, i, :].transpose([1, 0, 2]), in_=qbf[:])
            P.add("gpsimd", stq, dma=True, reads=["qbf"], writes=["qdrd"])
            dump(i, "dbg_q", qbf, "qbf")
            dump(i, "dbg_k", kst, "kst")
            dump(i, "dbg_v", vst, "vst")

        def build_block_b(i):
            def ldx2b(e, i=i):
                return e.dma_start(out=acc[:], in_=x2d[:, :, i, :].transpose([1, 0, 2]))
            P.add("sync", ldx2b, dma=True, reads=["x2dd"], writes=["acc"])
            def ldq(e, i=i):
                return e.dma_start(out=qbf[:], in_=qdr[:, :, i, :].transpose([1, 0, 2]))
            P.add("sync", ldq, dma=True, reads=["qdrd"], writes=["qbf"])
            attention(i)
            dump(i, "dbg_x3", acc, "acc")
            av = lambda dc: acc[:, dc, :]
            layernorm(av, "acc")
            router()
            ffn(i)
            def sto(e, i=i):
                dst = outap[:, :, i, :].transpose([1, 0, 2])
                return e.dma_start(out=dst, in_=acc[:])
            P.add("gpsimd", sto, dma=True, reads=["acc"], writes=["outd"])

        prologue()
        for i in range(NB):
            build_block_a(i)
        kscr_pass()
        if DEBUG_DUMPS:
            P.add("gpsimd", lambda e: e.dma_start(out=dbg["dbg_kdr"], in_=kdr),
                  dma=True, reads=["kdrd"], writes=["dbg_kdr"])
            def _dkscr(e):
                return e.dma_start(out=dbg["dbg_kscr"], in_=kscr[0:2, 0:4])
            P.add("gpsimd", _dkscr, dma=True, reads=["kscrd"], writes=["dbg_kscr"])
        for i in range(NB):
            build_block_b(i)

        P.resolve()

        import contextlib
        with contextlib.ExitStack() as es:
            sems = {}
            for ki, key in enumerate(P.sem_keys):
                sems[key] = es.enter_context(nc.semaphore(f"sm{ki}"))
            block = es.enter_context(nc.Block())

            @block.sync
            def _(e):
                P.emit("sync", e, sems)

            @block.scalar
            def _(e):
                P.emit("scalar", e, sems)

            @block.vector
            def _(e):
                P.emit("vector", e, sems)

            @block.tensor
            def _(e):
                P.emit("tensor", e, sems)

            @block.gpsimd
            def _(e):
                P.emit("gpsimd", e, sems)
                e.wait_ge(sems[("r", "outd")], P.dma_final["outd"])

    return nc, sbuf_left, len(P.ops)


_NC_CACHE = None


def kernel(L0_attn_x, L0_attn_qkv_weights, L0_ffn_router_W, L0_ffn_router_b,
           L0_ffn_W1, L0_ffn_b1, L0_ffn_W2, L0_ffn_b2):
    global _NC_CACHE
    x = np.asarray(L0_attn_x, dtype=np.float32)
    qkvw = np.asarray(L0_attn_qkv_weights, dtype=np.float32)
    rW = np.asarray(L0_ffn_router_W, dtype=np.float32)
    rb = np.asarray(L0_ffn_router_b, dtype=np.float32)
    W1 = np.asarray(L0_ffn_W1, dtype=np.float32)
    b1 = np.asarray(L0_ffn_b1, dtype=np.float32)
    W2 = np.asarray(L0_ffn_W2, dtype=np.float32)
    b2 = np.asarray(L0_ffn_b2, dtype=np.float32)

    if _NC_CACHE is None:
        _NC_CACHE = build_nc()
    nc, _, _ = _NC_CACHE

    qkvw_l = np.ascontiguousarray(qkvw.reshape(DC, 128, 3 * D))
    rwT_l = np.ascontiguousarray(rW.reshape(DC, 128, NE).transpose(1, 0, 2)).reshape(128, DC * NE)
    rb_l = np.ascontiguousarray(np.broadcast_to(rb[None, :], (128, NE)))
    b1T_l = np.ascontiguousarray(b1.reshape(NE, FCN, 128).transpose(2, 0, 1)).reshape(128, NE * FCN)
    b2T_l = np.ascontiguousarray(b2.reshape(NE, DC, 128).transpose(2, 0, 1)).reshape(128, NE * DC)
    w1_l = np.ascontiguousarray(W1.reshape(NE, DC, 128, DFF))
    w2_l = np.ascontiguousarray(W2.reshape(NE, FCN, 128, D))

    in_maps = []
    for b in range(8):
        x_fm = np.ascontiguousarray(x[b].T).reshape(DC, 128, NB, T)
        in_maps.append({
            "x_fm": x_fm, "qkvw": qkvw_l, "rwT": rwT_l, "rb_bc": rb_l,
            "b1T": b1T_l, "b2T": b2T_l, "w1": w1_l, "w2": w2_l,
        })
    res = run_bass_kernel_spmd(nc, in_maps, core_ids=list(range(8)))
    out = np.empty((8, S, D), np.float32)
    for b in range(8):
        o = res.results[b]["out_fm"]
        out[b] = o.transpose(2, 3, 0, 1).reshape(S, D)
    return out


# version 03
# speedup vs baseline: 1.0174x; 1.0174x over previous
import numpy as np
import ml_dtypes
import concourse.bass as bass
import concourse.mybir as mybir
from concourse.bass_utils import run_bass_kernel_spmd

F32 = mybir.dt.float32
BF16 = mybir.dt.bfloat16
AF = mybir.ActivationFunctionType
ALU = mybir.AluOpType
AX = mybir.AxisListType

D = 1024
S = 4096
NB = 8
T = 512
DC = 8
FCN = 32
NE = 8
NH = 16
DH = 64
NSUB = 4
DFF = 4096
EPS = 1e-5
EXP_SHIFT = -20.0

ENGINES = ("tensor", "vector", "scalar", "gpsimd", "sync")
DEBUG_DUMPS = False


class Prog:

    def __init__(self):
        self.ops = []
        self.last_writer = {}
        self.readers = {}

    def add(self, engine, fn, reads=(), writes=(), dma=False):
        idx = len(self.ops)
        deps = set()
        for r in reads:
            w = self.last_writer.get(r)
            if w is not None:
                deps.add(w)
        for w_ in writes:
            w = self.last_writer.get(w_)
            if w is not None:
                deps.add(w)
            for rd in self.readers.get(w_, ()):
                deps.add(rd)
        deps.discard(idx)
        wres = writes[0] if dma else None
        if dma:
            assert len(writes) == 1, "dma ops must write exactly one resource"
        self.ops.append(dict(engine=engine, fn=fn, dma=dma, deps=deps,
                             need_inc=False, idx=idx, wres=wres))
        for w_ in writes:
            self.last_writer[w_] = idx
            self.readers[w_] = []
        for r in reads:
            self.readers.setdefault(r, []).append(idx)
        return idx

    def resolve(self):
        for op in self.ops:
            kept = []
            for d in sorted(op["deps"]):
                p = self.ops[d]
                if p["engine"] == "tensor" and op["engine"] == "tensor" \
                        and not p["dma"] and not op["dma"]:
                    continue
                kept.append(d)
                if not p["dma"]:
                    p["need_inc"] = True
            op["edges"] = kept
        cc = {e: 0 for e in ENGINES}
        rc = {}
        for op in self.ops:
            e = op["engine"]
            if op["dma"]:
                r = op["wres"]
                rc[r] = rc.get(r, 0) + 1
                op["post"] = ("r", r, rc[r] * 16)
            elif op["need_inc"]:
                cc[e] += 1
                op["post"] = ("c", e, cc[e])
            else:
                op["post"] = None
        self.dma_final = {r: v * 16 for r, v in rc.items()}
        self.sem_keys = [("c", e) for e in ENGINES if e != "sync"] +                         [("r", r) for r in rc]
        know = {e: {} for e in ENGINES}
        for op in self.ops:
            k = know[op["engine"]]
            waits = {}
            for d in op["edges"]:
                p = self.ops[d]
                assert p["post"] is not None, (op["idx"], d)
                kind, pe, val = p["post"]
                sem = (kind, pe)
                if k.get(sem, 0) >= val:
                    continue
                if waits.get(sem, 0) < val:
                    waits[sem] = val
            for sem, val in waits.items():
                k[sem] = val
            op["waits"] = sorted(waits.items())
            if op["engine"] == "tensor" and op["post"] is not None and op["post"][0] == "c":
                k[("c", "tensor")] = op["post"][2]

    def emit(self, engine_name, eng, sems):
        for op in self.ops:
            if op["engine"] != engine_name:
                continue
            for (sem, val) in op["waits"]:
                eng.wait_ge(sems[sem], val)
            inst = op["fn"](eng)
            if op["dma"]:
                inst.then_inc(sems[("r", op["wres"])], 16)
            elif op["need_inc"]:
                if inst is None:
                    eng.nop().then_inc(sems[("c", engine_name)], 1)
                else:
                    inst.then_inc(sems[("c", engine_name)], 1)


def build_nc():
    nc = bass.Bass("TRN2", target_bir_lowering=False, debug=False)

    xin = nc.dram_tensor("x_fm", [DC, 128, NB, T], F32, kind="ExternalInput").ap()
    qkvw = nc.dram_tensor("qkvw", [DC, 128, 3 * D], F32, kind="ExternalInput").ap()
    rwT = nc.dram_tensor("rwT", [128, DC * NE], F32, kind="ExternalInput").ap()
    rbbc_in = nc.dram_tensor("rb_bc", [128, NE], F32, kind="ExternalInput").ap()
    b1T_in = nc.dram_tensor("b1T", [128, NE * FCN], F32, kind="ExternalInput").ap()
    b2T_in = nc.dram_tensor("b2T", [128, NE * DC], F32, kind="ExternalInput").ap()
    w1in = nc.dram_tensor("w1", [NE, DC, 128, DFF], F32, kind="ExternalInput").ap()
    w2in = nc.dram_tensor("w2", [NE, FCN, 128, D], F32, kind="ExternalInput").ap()
    outap = nc.dram_tensor("out_fm", [DC, 128, NB, T], F32, kind="ExternalOutput").ap()
    dbg = {}
    if DEBUG_DUMPS:
        for nm, shape, dt in [("dbg_normed1", [128, DC, T], BF16),
                              ("dbg_x2", [128, DC, T], F32),
                              ("dbg_gbc", [128, NE, T], BF16),
                              ("dbg_q", [128, DC, T], BF16),
                              ("dbg_k", [128, DC, T], BF16),
                              ("dbg_v", [128, NSUB, NH * 65], BF16),
                              ("dbg_x3", [128, DC, T], F32),
                              ("dbg_h", [128, FCN * T], BF16)]:
            dbg[nm] = nc.dram_tensor(nm, shape, dt, kind="ExternalOutput").ap()
        dbg["dbg_kdr"] = nc.dram_tensor("dbg_kdr", [DC, 128, S], BF16,
                                        kind="ExternalOutput").ap()
        dbg["dbg_kscr"] = nc.dram_tensor("dbg_kscr", [2, 4, 128, 128], BF16,
                                         kind="ExternalOutput").ap()

    w1b = [nc.dram_tensor(f"w1b{e}", [FCN, 128, DC, 128], BF16, kind="Internal").ap()
           for e in range(NE)]
    w2b = [nc.dram_tensor(f"w2b{e}", [FCN, 128, D], BF16, kind="Internal").ap()
           for e in range(NE)]
    qkvb = nc.dram_tensor("qkvb", [DC, 128, 3 * D], BF16, kind="Internal").ap()
    gsc = nc.dram_tensor("gsc", [1, NE, T], BF16, kind="Internal").ap()
    kdr = nc.dram_tensor("kdr", [DC, 128, S], BF16, kind="Internal").ap()
    vdr = nc.dram_tensor("vdr", [S, NH * 65], BF16, kind="Internal").ap()
    x2d = nc.dram_tensor("x2d", [DC, 128, NB, T], F32, kind="Internal").ap()
    qdr = nc.dram_tensor("qdr", [DC, 128, NB, T], BF16, kind="Internal").ap()
    kscr = nc.dram_tensor("kscr", [NH // 2, FCN, 128, 128], BF16, kind="Internal").ap()

    sb = {}

    def sbuf(name, shape, dt):
        sb[name] = nc.alloc_sbuf_tensor(name, shape, dt)
        return sb[name]

    rwt = sbuf("rwt", [128, DC * NE], BF16)
    rbbc = sbuf("rbbc", [128, NE], F32)
    b1t = sbuf("b1t", [128, NE * FCN], F32)
    b2t = sbuf("b2t", [128, NE * DC], F32)
    iden = sbuf("iden", [128, 128], F32)
    iotap = sbuf("iotap", [128, 1], F32)
    iota8 = sbuf("iota8", [128, 8], F32)
    iom9 = sbuf("iom9", [128, 8], F32)
    onesc = sbuf("onesc", [128, 1], F32)
    ones1 = sbuf("ones1", [1, 128], F32)
    masks = sbuf("masks", [128, 4, T], BF16)
    iotaq = sbuf("iotaq", [128, T], F32)

    xt = sbuf("xt", [128, 2, DC, T], F32)
    acc = sbuf("acc", [128, DC, T], F32)
    normed = sbuf("normed", [128, DC, T], BF16)
    hbuf = sbuf("hbuf", [128, FCN * T], BF16)
    sqv = sb["hbuf"].bitcast(F32)
    gbc = sbuf("gbc", [128, NE, T], BF16)
    qbf = sbuf("qbf", [128, DC, T], BF16)
    kst = sbuf("kst", [128, DC, T], BF16)
    vst = sbuf("vst", [128, NSUB, NH * 65], BF16)

    w1r = [sbuf(f"w1r{s}", [128, DC, 128], BF16) for s in range(4)]
    w2r = [sbuf(f"w2r{s}", [128, DC, 128], BF16) for s in range(4)]
    qkr = [sbuf(f"qkr{s}", [128, 1024], BF16) for s in range(2)]
    ksrng = [sbuf(f"ksrng{s}", [128, 128], BF16) for s in range(6)]
    katile = [sbuf(f"katile{s}", [128, 128], BF16) for s in range(2)]
    kstg = [sbuf(f"kstg{s}", [128, 128], BF16) for s in range(2)]
    iden_bf = sbuf("iden_bf", [128, 128], BF16)
    vrng = [sbuf(f"vrng{s}", [128, NH * 65], BF16) for s in range(4)]
    erng = [sbuf(f"erng{s}", [128, T], BF16) for s in range(4)]
    cstg = [sbuf(f"cstg{s}", [128, 2048], BF16) for s in range(2)]

    rows = sbuf("rows", [1, 7 * T], F32)
    grow8 = sbuf("grow8", [1, NE * T], BF16)
    eps1 = sbuf("eps1", [1, 1], F32)
    shf = sbuf("shf", [128, 1], F32)
    ones1b = sbuf("ones1b", [1, 128], BF16)
    a_sb = sbuf("a_sb", [128, T], F32)
    b_sb = sbuf("b_sb", [128, T], F32)
    rsbc = sbuf("rsbc", [128, T], F32)
    tmp1 = sbuf("tmp1", [128, T], F32)
    tmp2 = sbuf("tmp2", [128, T], F32)
    lgt = sbuf("lgt", [128, 8], F32)
    lmax = sbuf("lmax", [128, 1], F32)
    negm = sbuf("negm", [128, 1], F32)
    expv = sbuf("expv", [128, 8], F32)
    sume = sbuf("sume", [128, 1], F32)
    gval = sbuf("gval", [128, 1], F32)
    meq = sbuf("meq", [128, 8], F32)
    selv = sbuf("selv", [128, 8], F32)
    emin = sbuf("emin", [128, 1], F32)
    g8 = sbuf("g8", [128, 8], F32)
    grs = sbuf("grs", [8, T], F32)

    sbuf_left = nc.sbuf_bytes_remaining
    assert sbuf_left >= 0, sbuf_left

    R_MU, R_MSQ, R_VAR, R_LNV, R_A, R_B, R_RS = range(7)

    def row(i):
        return sb["rows"][0:1, i * T:(i + 1) * T]

    P = Prog()

    with (
        nc.psum_tensor("pb0", [128, T], F32) as pb0,
        nc.psum_tensor("pb1", [128, T], F32) as pb1,
        nc.psum_tensor("pb2", [128, T], F32) as pb2,
        nc.psum_tensor("pb3", [128, T], F32) as pb3,
        nc.psum_tensor("pb4", [128, T], F32) as pb4,
        nc.psum_tensor("pb5", [128, T], F32) as pb5,
        nc.psum_tensor("pb6", [128, T], F32) as pb6,
        nc.psum_tensor("pb7", [128, T], F32) as pb7,
    ):
        pb = [pb0, pb1, pb2, pb3, pb4, pb5, pb6, pb7]
        pname = [f"pb{i}" for i in range(8)]

        def prologue():
            P.add("gpsimd", lambda e: e.iota(iotap[:], [[0, 1]], channel_multiplier=1,
                                             allow_small_or_imprecise_dtypes=True),
                  writes=["iotap"])
            P.add("gpsimd", lambda e: e.iota(iota8[:], [[1, 8]], channel_multiplier=0,
                                             allow_small_or_imprecise_dtypes=True),
                  writes=["iota8"])
            P.add("gpsimd", lambda e: e.iota(iotaq[:], [[1, T]], channel_multiplier=0,
                                             allow_small_or_imprecise_dtypes=True),
                  writes=["iotaq"])
            P.add("vector", lambda e: e.tensor_scalar(iom9[:], iota8[:], -9.0, None,
                                                      ALU.add),
                  reads=["iota8"], writes=["iom9"])
            P.add("vector", lambda e: e.memset(onesc[:], 1.0), writes=["onesc"])
            P.add("vector", lambda e: e.memset(ones1[:], 1.0), writes=["ones1"])
            P.add("vector", lambda e: e.memset(ones1b[:], 1.0), writes=["ones1b"])
            P.add("vector", lambda e: e.memset(eps1[:], EPS), writes=["eps1"])
            P.add("vector", lambda e: e.memset(shf[:], EXP_SHIFT), writes=["shf"])
            P.add("vector", lambda e: e.memset(vst[:], 1.0), writes=["vst"])
            P.add("gpsimd", lambda e: e.iota(iden[:], [[1, 128]], channel_multiplier=0,
                                             allow_small_or_imprecise_dtypes=True),
                  writes=["iden"])
            P.add("vector", lambda e: e.tensor_scalar(iden[:], iden[:], iotap[:], None,
                                                      ALU.is_equal),
                  reads=["iotap"], writes=["iden"])
            P.add("vector", lambda e: e.tensor_copy(iden_bf[:], iden[:]),
                  reads=["iden"], writes=["iden_bf"])
            for m in range(4):
                def mk(e, m=m):
                    return e.tensor_scalar(negm[:], iotap[:], float(128 * m), None, ALU.add)
                P.add("vector", mk, reads=["iotap"], writes=["negm"])
                def mk2(e, m=m):
                    return e.tensor_scalar(masks[:, m, :], iotaq[:], negm[:], None, ALU.is_ge)
                P.add("vector", mk2, reads=["iotaq", "negm"], writes=["masks"])
            P.add("gpsimd", lambda e: e.dma_start(out=rwt[:], in_=rwT), dma=True,
                  writes=["rwt"])
            P.add("sync", lambda e: e.dma_start(out=rbbc[:], in_=rbbc_in), dma=True,
                  writes=["rbbc"])
            P.add("sync", lambda e: e.dma_start(out=b1t[:], in_=b1T_in), dma=True,
                  writes=["b1t"])
            P.add("sync", lambda e: e.dma_start(out=b2t[:], in_=b2T_in), dma=True,
                  writes=["b2t"])
            slc = [0]
            for e_ in range(NE):
                for fs2 in range(FCN // 2):
                    sl = slc[0] % 2
                    slc[0] += 1
                    def ld(eng, e_=e_, fs2=fs2, sl=sl):
                        src = w1in[e_, :, :, fs2 * 256:(fs2 + 1) * 256].transpose(
                            [1, 0, 2])
                        return eng.dma_start(out=cstg[sl][:].rearrange(
                            "p (dc f) -> p dc f", dc=DC), in_=src)
                    P.add("gpsimd", ld, dma=True, writes=[f"cstg{sl}"])
                    def st(eng, e_=e_, fs2=fs2, sl=sl):
                        dst = w1b[e_][2 * fs2:2 * fs2 + 2].transpose([1, 2, 0, 3])
                        return eng.dma_start(out=dst, in_=cstg[sl][:].rearrange(
                            "p (dc fcp f) -> p dc fcp f", dc=DC, fcp=2))
                    P.add("sync", st, dma=True, reads=[f"cstg{sl}"], writes=[f"w1bd{e_}"])
                for fs2 in range(FCN // 2):
                    sl = slc[0] % 2
                    slc[0] += 1
                    def ld2(eng, e_=e_, fs2=fs2, sl=sl):
                        src = w2in[e_, 2 * fs2:2 * fs2 + 2, :, :].transpose([1, 0, 2])
                        return eng.dma_start(out=cstg[sl][:].rearrange(
                            "p (fcp d) -> p fcp d", fcp=2), in_=src)
                    P.add("gpsimd", ld2, dma=True, writes=[f"cstg{sl}"])
                    def st2(eng, e_=e_, fs2=fs2, sl=sl):
                        dst = w2b[e_][2 * fs2:2 * fs2 + 2].transpose([1, 0, 2])
                        return eng.dma_start(out=dst, in_=cstg[sl][:].rearrange(
                            "p (fcp d) -> p fcp d", fcp=2))
                    P.add("sync", st2, dma=True, reads=[f"cstg{sl}"], writes=[f"w2bd{e_}"])
            for dc in range(DC):
                for qs in range(3):
                    sl = slc[0] % 2
                    slc[0] += 1
                    def ld3(eng, dc=dc, qs=qs, sl=sl):
                        return eng.dma_start(out=cstg[sl][:, 0:1024],
                                             in_=qkvw[dc][:, qs * 1024:(qs + 1) * 1024])
                    P.add("gpsimd", ld3, dma=True, writes=[f"cstg{sl}"])
                    def st3(eng, dc=dc, qs=qs, sl=sl):
                        return eng.dma_start(out=qkvb[dc][:, qs * 1024:(qs + 1) * 1024],
                                             in_=cstg[sl][:, 0:1024])
                    P.add("sync", st3, dma=True, reads=[f"cstg{sl}"], writes=["qkvbd"])

        def layernorm(src_view, src_res):
            for dc in range(DC):
                def sq(e, dc=dc):
                    return e.activation(sqv[:, dc * T:(dc + 1) * T], src_view(dc), AF.Square)
                P.add("scalar", sq, reads=[src_res], writes=["hbuf"])
            for dc in range(DC):
                def mm(e, dc=dc):
                    return e.matmul(pb6[0:1, :], onesc[:], src_view(dc),
                                    start=(dc == 0), stop=(dc == DC - 1))
                P.add("tensor", mm, reads=[src_res, "onesc"], writes=["pb6"])
            for dc in range(DC):
                def mm2(e, dc=dc):
                    return e.matmul(pb7[0:1, :], onesc[:], sqv[:, dc * T:(dc + 1) * T],
                                    start=(dc == 0), stop=(dc == DC - 1))
                P.add("tensor", mm2, reads=["hbuf", "onesc"], writes=["pb7"])
            P.add("vector", lambda e: e.tensor_scalar(row(R_MU), pb6[0:1, :], 1.0 / D, None,
                                                      ALU.mult),
                  reads=["pb6"], writes=["rows"])
            P.add("vector", lambda e: e.tensor_scalar(row(R_MSQ), pb7[0:1, :], 1.0 / D, None,
                                                      ALU.mult),
                  reads=["pb7"], writes=["rows"])
            P.add("vector", lambda e: e.scalar_tensor_tensor(row(R_VAR), row(R_MU), -1.0,
                                                             row(R_MU), ALU.mult, ALU.mult),
                  reads=["rows"], writes=["rows"])
            P.add("vector", lambda e: e.tensor_tensor(row(R_VAR), row(R_VAR), row(R_MSQ),
                                                      ALU.add),
                  reads=["rows"], writes=["rows"])
            P.add("scalar", lambda e: e.activation(row(R_LNV), row(R_VAR), AF.Ln,
                                                   bias=eps1[:]),
                  reads=["rows", "eps1"], writes=["rows"])
            P.add("scalar", lambda e: e.activation(row(R_A), row(R_LNV), AF.Exp, scale=-0.5),
                  reads=["rows"], writes=["rows"])
            P.add("vector", lambda e: e.scalar_tensor_tensor(row(R_B), row(R_MU), -1.0,
                                                             row(R_A), ALU.mult, ALU.mult),
                  reads=["rows"], writes=["rows"])
            P.add("tensor", lambda e: e.matmul(pb6[:], ones1[:], row(R_A), start=True,
                                               stop=True),
                  reads=["rows", "ones1"], writes=["pb6"])
            P.add("tensor", lambda e: e.matmul(pb7[:], ones1[:], row(R_B), start=True,
                                               stop=True),
                  reads=["rows", "ones1"], writes=["pb7"])
            P.add("vector", lambda e: e.tensor_copy(a_sb[:], pb6[:]),
                  reads=["pb6"], writes=["a_sb"])
            P.add("vector", lambda e: e.tensor_copy(b_sb[:], pb7[:]),
                  reads=["pb7"], writes=["b_sb"])
            for dc in range(DC):
                def n1(e, dc=dc):
                    return e.tensor_tensor(tmp1[:], src_view(dc), a_sb[:], ALU.mult)
                P.add("vector", n1, reads=[src_res, "a_sb"], writes=["tmp1"])
                def n2(e, dc=dc):
                    return e.tensor_tensor(normed[:, dc, :], tmp1[:], b_sb[:], ALU.add)
                P.add("vector", n2, reads=["tmp1", "b_sb"], writes=["normed"])

        def router():
            tbanks = [2, 3, 4, 5]
            for sub in range(NSUB):
                for dc in range(DC):
                    def mm(e, sub=sub, dc=dc):
                        return e.matmul(pb6[:, 0:8],
                                        normed[:, dc, sub * 128:(sub + 1) * 128],
                                        rwt[:, dc * 8:(dc + 1) * 8],
                                        start=(dc == 0), stop=(dc == DC - 1))
                    P.add("tensor", mm, reads=["normed", "rwt"], writes=["pb6"])
                P.add("vector", lambda e: e.tensor_tensor(lgt[:], pb6[:, 0:8], rbbc[:],
                                                          ALU.add),
                      reads=["pb6", "rbbc"], writes=["lgt"])
                P.add("vector", lambda e: e.tensor_reduce(lmax[:], lgt[:], AX.X, ALU.max),
                      reads=["lgt"], writes=["lmax"])
                P.add("vector", lambda e: e.tensor_scalar(negm[:], lmax[:], -1.0, None,
                                                          ALU.mult),
                      reads=["lmax"], writes=["negm"])
                P.add("scalar", lambda e: e.activation(expv[:], lgt[:], AF.Exp,
                                                       bias=negm[:]),
                      reads=["lgt", "negm"], writes=["expv"])
                P.add("vector", lambda e: e.tensor_reduce(sume[:], expv[:], AX.X, ALU.add),
                      reads=["expv"], writes=["sume"])
                P.add("vector", lambda e: e.reciprocal(gval[:], sume[:]),
                      reads=["sume"], writes=["gval"])
                P.add("vector", lambda e: e.tensor_scalar(gval[:], gval[:], 0.5, None,
                                                          ALU.mult),
                      reads=["gval"], writes=["gval"])
                P.add("vector", lambda e: e.tensor_scalar(meq[:], lgt[:], lmax[:], None,
                                                          ALU.is_equal),
                      reads=["lgt", "lmax"], writes=["meq"])
                P.add("vector", lambda e: e.tensor_tensor(selv[:], meq[:], iom9[:],
                                                          ALU.mult),
                      reads=["meq", "iom9"], writes=["selv"])
                P.add("vector", lambda e: e.tensor_scalar(selv[:], selv[:], 9.0, None,
                                                          ALU.add),
                      reads=["selv"], writes=["selv"])
                P.add("vector", lambda e: e.tensor_reduce(emin[:], selv[:], AX.X, ALU.min),
                      reads=["selv"], writes=["emin"])
                P.add("vector", lambda e: e.tensor_scalar(meq[:], iota8[:], emin[:], None,
                                                          ALU.is_equal),
                      reads=["iota8", "emin"], writes=["meq"])
                P.add("vector", lambda e: e.tensor_scalar(g8[:], meq[:], gval[:], None,
                                                          ALU.mult),
                      reads=["meq", "gval"], writes=["g8"])
                bk = tbanks[sub]
                def tr(e, sub=sub, bk=bk):
                    return e.transpose(pb[bk][0:8, 0:128], g8[:], iden[:])
                P.add("tensor", tr, reads=["g8", "iden"], writes=[pname[bk]])
                def ev(e, sub=sub, bk=bk):
                    return e.tensor_copy(grs[:, sub * 128:(sub + 1) * 128],
                                         pb[bk][0:8, 0:128])
                P.add("vector", ev, reads=[pname[bk]], writes=["grs"])
            P.add("gpsimd", lambda e: e.dma_start(out=gsc[0], in_=grs[:]), dma=True,
                  reads=["grs"], writes=["gscd"])
            def mvrow(e):
                dst = grow8[0:1, :].rearrange("o (e t) -> o e t", e=8)
                return e.dma_start(out=dst, in_=gsc)
            P.add("sync", mvrow, dma=True, reads=["gscd"], writes=["grow8"])
            for e_ in range(NE):
                bk = 6 if e_ % 2 == 0 else 7
                def mm(e, e_=e_, bk=bk):
                    return e.matmul(pb[bk][:], ones1b[:],
                                    grow8[0:1, e_ * T:(e_ + 1) * T], start=True, stop=True)
                P.add("tensor", mm, reads=["grow8", "ones1b"], writes=[pname[bk]])
                def ev2(e, e_=e_, bk=bk):
                    return e.tensor_copy(gbc[:, e_, :], pb[bk][:])
                P.add("vector", ev2, reads=[pname[bk]], writes=["gbc"])

        def ffn(iblk):
            def ld_w1(e_, fc):
                sl = fc % 4
                def ld(eng, e_=e_, fc=fc, sl=sl):
                    return eng.dma_start(out=w1r[sl][:], in_=w1b[e_][fc])
                P.add("sync", ld, dma=True, reads=[f"w1bd{e_}"], writes=[f"w1r{sl}"])

            def ld_w2(e_, fc):
                sl = fc % 4
                def ld2(eng, e_=e_, fc=fc, sl=sl):
                    return eng.dma_start(out=w2r[sl][:].rearrange(
                        "p dc f -> p (dc f)"), in_=w2b[e_][fc])
                P.add("sync", ld2, dma=True, reads=[f"w2bd{e_}"], writes=[f"w2r{sl}"])

            for e_ in range(NE):
                for fc in range(4):
                    ld_w1(e_, fc)
                for fc in range(FCN):
                    sl = fc % 4
                    hb = 6 if fc % 2 == 0 else 7
                    for dc in range(DC):
                        def mm(e, fc=fc, dc=dc, sl=sl, hb=hb):
                            return e.matmul(pb[hb][:], w1r[sl][:, dc, :], normed[:, dc, :],
                                            start=(dc == 0), stop=(dc == DC - 1))
                        P.add("tensor", mm, reads=[f"w1r{sl}", "normed"], writes=[pname[hb]])
                    if fc + 4 < FCN:
                        ld_w1(e_, fc + 4)
                    def relu(e, e_=e_, fc=fc, hb=hb):
                        col = e_ * FCN + fc
                        return e.activation(hbuf[:, fc * T:(fc + 1) * T], pb[hb][:],
                                            AF.Relu, bias=b1t[:, col:col + 1])
                    P.add("scalar", relu, reads=[pname[hb], "b1t"], writes=["hbuf"])
                for fc in range(4):
                    ld_w2(e_, fc)
                for half in range(2):
                    for fc in range(FCN):
                        sl = fc % 4
                        for dci in range(4):
                            dc = half * 4 + dci
                            yb = 2 + dci
                            def mm2(e, fc=fc, dc=dc, sl=sl, yb=yb):
                                return e.matmul(pb[yb][:], w2r[sl][:, dc, :],
                                                hbuf[:, fc * T:(fc + 1) * T],
                                                start=(fc == 0), stop=(fc == FCN - 1))
                            P.add("tensor", mm2, reads=[f"w2r{sl}", "hbuf"],
                                  writes=[pname[yb]])
                        if fc + 4 < FCN:
                            ld_w2(e_, fc + 4)
                    if half == 0:
                        for fc in range(4):
                            ld_w2(e_, fc)
                    for dci in range(4):
                        dc = half * 4 + dci
                        yb = 2 + dci
                        def ev(e, e_=e_, dc=dc, yb=yb):
                            col = e_ * DC + dc
                            return e.scalar_tensor_tensor(tmp2[:], pb[yb][:],
                                                          b2t[:, col:col + 1],
                                                          gbc[:, e_, :], ALU.add, ALU.mult)
                        P.add("vector", ev, reads=[pname[yb], "b2t", "gbc"], writes=["tmp2"])
                        def ac(e, dc=dc):
                            return e.tensor_tensor(acc[:, dc, :], acc[:, dc, :], tmp2[:],
                                                   ALU.add)
                        P.add("vector", ac, reads=["tmp2", "acc"], writes=["acc"])

        def qkv(iblk):
            ocgroups = [list(range(0, 6)), list(range(6, 12)), list(range(12, 16))]
            for grp in ocgroups:
                base = grp[0] * 128
                width = len(grp) * 128
                for dc in range(DC):
                    sl = dc % 2
                    def ld(eng, dc=dc, sl=sl, base=base, width=width):
                        return eng.dma_start(out=qkr[sl][:, 0:width],
                                             in_=qkvb[dc][:, base:base + width])
                    P.add("sync", ld, dma=True, reads=["qkvbd"], writes=[f"qkr{sl}"])
                    for k, oc in enumerate(grp):
                        def mm(e, dc=dc, sl=sl, k=k):
                            return e.matmul(pb[k][:], qkr[sl][:, k * 128:(k + 1) * 128],
                                            normed[:, dc, :], start=(dc == 0),
                                            stop=(dc == DC - 1))
                        P.add("tensor", mm, reads=[f"qkr{sl}", "normed"], writes=[pname[k]])
                for k, oc in enumerate(grp):
                    if oc < 8:
                        def ev(e, k=k, oc=oc):
                            return e.tensor_copy(qbf[:, oc, :], pb[k][:])
                        P.add("vector", ev, reads=[pname[k]], writes=["qbf"])
                    else:
                        def ev(e, k=k, oc=oc):
                            return e.tensor_copy(kst[:, oc - 8, :], pb[k][:])
                        P.add("vector", ev, reads=[pname[k]], writes=["kst"])
            for grp in ([(0, 0), (0, 1), (1, 0), (1, 1)], [(2, 0), (2, 1), (3, 0), (3, 1)]):
                for dc in range(DC):
                    sl = dc % 2
                    def ld2(eng, dc=dc, sl=sl):
                        return eng.dma_start(out=qkr[sl][:], in_=qkvb[dc][:, 2 * D:3 * D])
                    P.add("sync", ld2, dma=True, reads=["qkvbd"], writes=[f"qkr{sl}"])
                    for k, (sub, vh) in enumerate(grp):
                        def mm(e, dc=dc, sl=sl, sub=sub, vh=vh, k=k):
                            return e.matmul(pb[2 + k][:],
                                            normed[:, dc, sub * 128:(sub + 1) * 128],
                                            qkr[sl][:, vh * 512:(vh + 1) * 512],
                                            start=(dc == 0), stop=(dc == DC - 1))
                        P.add("tensor", mm, reads=[f"qkr{sl}", "normed"],
                              writes=[pname[2 + k]])
                for k, (sub, vh) in enumerate(grp):
                    def ev2(e, k=k, sub=sub, vh=vh):
                        dst = vst[:, sub, :].rearrange("p (h c) -> p h c", c=65)
                        dst = dst[:, 8 * vh:8 * vh + 8, 0:64]
                        src = pb[2 + k][:].rearrange("p (h c) -> p h c", c=64)
                        return e.tensor_copy(dst, src)
                    P.add("vector", ev2, reads=[pname[2 + k]], writes=["vst"])
            def stk(e, iblk=iblk):
                dst = kdr[:, :, iblk * T:(iblk + 1) * T].transpose([1, 0, 2])
                return e.dma_start(out=dst, in_=kst[:].rearrange("p dc t -> p dc t"))
            P.add("gpsimd", stk, dma=True, reads=["kst"], writes=["kdrd"])
            def stv(e, iblk=iblk):
                dst = vdr[iblk * T:(iblk + 1) * T, :].rearrange("(sub p) c -> p sub c", p=128)
                return e.dma_start(out=dst, in_=vst[:])
            P.add("gpsimd", stv, dma=True, reads=["vst"], writes=["vdrd"])

        def kscr_pass():
            cnt = [0]
            for hp in range(NH // 2):
                for jc in range(FCN):
                    sl = cnt[0] % 2
                    hb = 6 if cnt[0] % 2 == 0 else 7
                    cnt[0] += 1
                    def ld(e, hp=hp, jc=jc, sl=sl):
                        return e.dma_start(
                            out=katile[sl][:],
                            in_=kdr[hp, :, :].rearrange(
                                "(hh pr) (m c) -> hh pr m c", hh=2, c=64)[
                                :, 2 * jc:2 * jc + 2, :, :].transpose([1, 2, 0, 3]))
                    P.add("scalar", ld, dma=True, reads=["kdrd"], writes=[f"katile{sl}"])
                    def tr(e, sl=sl, hb=hb):
                        out_bf = pb[hb].bitcast(BF16)[0:128, 0:128]
                        return e.transpose(out_bf, katile[sl][:], iden_bf[:])
                    P.add("tensor", tr, reads=[f"katile{sl}", "iden_bf"], writes=[pname[hb]])
                    def ev(e, sl=sl, hb=hb):
                        return e.tensor_copy(kstg[sl][:], pb[hb].bitcast(BF16)[0:128, 0:128])
                    P.add("vector", ev, reads=[pname[hb]], writes=[f"kstg{sl}"])
                    def st(e, hp=hp, jc=jc, sl=sl):
                        return e.dma_start(out=kscr[hp, jc], in_=kstg[sl][:])
                    P.add("gpsimd", st, dma=True, reads=[f"kstg{sl}"], writes=["kscrd"])

        def attention(iblk):
            jlast = 4 * iblk + 3
            passes = [list(range(0, 6)), list(range(6, 12)), list(range(12, 16))]
            for heads in passes:
                npair = len(heads) // 2
                base = heads[0]
                kcnt = [0]
                kslot = {}
                def ldkv(j, pairs_hp):
                    sl = j % 4
                    def f2(e, j=j, sl=sl):
                        return e.dma_start(out=vrng[sl][:], in_=vdr[j * 128:(j + 1) * 128, :])
                    P.add("scalar", f2, dma=True, reads=["vdrd"], writes=[f"vrng{sl}"])
                    for hp in pairs_hp:
                        ks = kcnt[0] % 6
                        kcnt[0] += 1
                        kslot[(j, hp)] = ks
                        def f3(e, j=j, hp=hp, ks=ks):
                            return e.dma_start(out=ksrng[ks][:], in_=kscr[hp, j])
                        P.add("scalar", f3, dma=True, reads=["kscrd"],
                              writes=[f"ksrng{ks}"])
                pairs_hp = [(base // 2) + pi for pi in range(npair)]
                ldkv(0, pairs_hp)
                esl = 0
                for j in range(jlast + 1):
                    if j + 1 <= jlast:
                        ldkv(j + 1, pairs_hp)
                    sl = j % 4
                    m = j - 4 * iblk
                    for pi in range(npair):
                        c = (base // 2) + pi
                        hA, hB = 2 * c, 2 * c + 1
                        eA, eB = esl % 4, (esl + 1) % 4
                        esl += 2
                        ks = kslot[(j, c)]
                        def mmA(e, c=c, ks=ks):
                            return e.matmul(pb6[:], ksrng[ks][0:64, :],
                                            qbf[0:64, c, :], start=True, stop=True)
                        P.add("tensor", mmA, reads=[f"ksrng{ks}", "qbf"], writes=["pb6"])
                        def mmB(e, c=c, ks=ks):
                            return e.matmul(pb7[:], ksrng[ks][64:128, :],
                                            qbf[64:128, c, :], start=True, stop=True)
                        P.add("tensor", mmB, reads=[f"ksrng{ks}", "qbf"], writes=["pb7"])
                        def exA(e, eA=eA):
                            return e.activation(erng[eA][:], pb6[:], AF.Exp, scale=0.125,
                                                bias=shf[:])
                        P.add("scalar", exA, reads=["pb6", "shf"], writes=[f"erng{eA}"])
                        def exB(e, eB=eB):
                            return e.activation(erng[eB][:], pb7[:], AF.Exp, scale=0.125,
                                                bias=shf[:])
                        P.add("scalar", exB, reads=["pb7", "shf"], writes=[f"erng{eB}"])
                        if 0 <= m <= 3:
                            def mkA(e, eA=eA, m=m):
                                return e.tensor_tensor(erng[eA][:], erng[eA][:],
                                                       masks[:, m, :], ALU.mult)
                            P.add("vector", mkA, reads=[f"erng{eA}", "masks"],
                                  writes=[f"erng{eA}"])
                            def mkB(e, eB=eB, m=m):
                                return e.tensor_tensor(erng[eB][:], erng[eB][:],
                                                       masks[:, m, :], ALU.mult)
                            P.add("vector", mkB, reads=[f"erng{eB}", "masks"],
                                  writes=[f"erng{eB}"])
                        bA = hA - base
                        bB = hB - base
                        def pvA(e, hA=hA, sl=sl, eA=eA, bA=bA, j=j):
                            return e.matmul(pb[bA][0:65, :],
                                            vrng[sl][:, 65 * hA:65 * hA + 65],
                                            erng[eA][:], start=(j == 0), stop=(j == jlast))
                        P.add("tensor", pvA, reads=[f"vrng{sl}", f"erng{eA}"],
                              writes=[pname[bA]])
                        def pvB(e, hB=hB, sl=sl, eB=eB, bB=bB, j=j):
                            return e.matmul(pb[bB][0:65, :],
                                            vrng[sl][:, 65 * hB:65 * hB + 65],
                                            erng[eB][:], start=(j == 0), stop=(j == jlast))
                        P.add("tensor", pvB, reads=[f"vrng{sl}", f"erng{eB}"],
                              writes=[pname[bB]])
                for h in heads:
                    bk = h - base
                    def rc(e, bk=bk):
                        return e.reciprocal(row(R_RS), pb[bk][64:65, :])
                    P.add("vector", rc, reads=[pname[bk]], writes=["rows"])
                    P.add("tensor", lambda e: e.matmul(pb6[:], ones1[:], row(R_RS),
                                                       start=True, stop=True),
                          reads=["rows", "ones1"], writes=["pb6"])
                    P.add("vector", lambda e: e.tensor_copy(rsbc[:], pb6[:]),
                          reads=["pb6"], writes=["rsbc"])
                    po = 64 * (h % 2)
                    dc = h // 2
                    def fin1(e, bk=bk, po=po):
                        return e.tensor_tensor(tmp1[po:po + 64, :], pb[bk][0:64, :],
                                               rsbc[po:po + 64, :], ALU.mult)
                    P.add("vector", fin1, reads=[pname[bk], "rsbc"], writes=["tmp1"])
                    def fin2(e, dc=dc, po=po):
                        return e.tensor_tensor(acc[po:po + 64, dc, :],
                                               acc[po:po + 64, dc, :],
                                               tmp1[po:po + 64, :], ALU.add)
                    P.add("vector", fin2, reads=["tmp1", "acc"], writes=["acc"])

        def dump(i, name, tile, res):
            if DEBUG_DUMPS and i == 0:
                P.add("gpsimd", lambda e: e.dma_start(out=dbg[name], in_=tile[:]),
                      dma=True, reads=[res], writes=[name])

        def build_block_a(i):
            s = i % 2
            if i == 0:
                def ldx(e):
                    return e.dma_start(out=xt[:, 0, :, :], in_=xin[:, :, 0, :].transpose(
                        [1, 0, 2]))
                P.add("sync", ldx, dma=True, writes=["xt0"])
            if i + 1 < NB:
                def ldx2(e, i=i):
                    return e.dma_start(out=xt[:, (i + 1) % 2, :, :],
                                       in_=xin[:, :, i + 1, :].transpose([1, 0, 2]))
                P.add("sync", ldx2, dma=True, writes=[f"xt{(i + 1) % 2}"])

            xres = f"xt{s}"
            xv = lambda dc, s=s: xt[:, s, dc, :]
            layernorm(xv, xres)
            dump(i, "dbg_normed1", normed, "normed")
            router()
            dump(i, "dbg_gbc", gbc, "gbc")
            for dc in range(DC):
                def cp(e, dc=dc, s=s):
                    return e.tensor_copy(acc[:, dc, :], xt[:, s, dc, :])
                P.add("vector", cp, reads=[xres], writes=["acc"])
            ffn(i)
            dump(i, "dbg_h", sb["hbuf"], "hbuf")
            dump(i, "dbg_x2", acc, "acc")
            av = lambda dc: acc[:, dc, :]
            layernorm(av, "acc")
            def stx2(e, i=i):
                return e.dma_start(out=x2d[:, :, i, :].transpose([1, 0, 2]), in_=acc[:])
            P.add("gpsimd", stx2, dma=True, reads=["acc"], writes=["x2dd"])
            qkv(i)
            def stq(e, i=i):
                return e.dma_start(out=qdr[:, :, i, :].transpose([1, 0, 2]), in_=qbf[:])
            P.add("gpsimd", stq, dma=True, reads=["qbf"], writes=["qdrd"])
            dump(i, "dbg_q", qbf, "qbf")
            dump(i, "dbg_k", kst, "kst")
            dump(i, "dbg_v", vst, "vst")

        def build_block_b(i):
            def ldx2b(e, i=i):
                return e.dma_start(out=acc[:], in_=x2d[:, :, i, :].transpose([1, 0, 2]))
            P.add("sync", ldx2b, dma=True, reads=["x2dd"], writes=["acc"])
            def ldq(e, i=i):
                return e.dma_start(out=qbf[:], in_=qdr[:, :, i, :].transpose([1, 0, 2]))
            P.add("sync", ldq, dma=True, reads=["qdrd"], writes=["qbf"])
            attention(i)
            dump(i, "dbg_x3", acc, "acc")
            av = lambda dc: acc[:, dc, :]
            layernorm(av, "acc")
            router()
            ffn(i)
            def sto(e, i=i):
                dst = outap[:, :, i, :].transpose([1, 0, 2])
                return e.dma_start(out=dst, in_=acc[:])
            P.add("gpsimd", sto, dma=True, reads=["acc"], writes=["outd"])

        prologue()
        for i in range(NB):
            build_block_a(i)
        kscr_pass()
        if DEBUG_DUMPS:
            P.add("gpsimd", lambda e: e.dma_start(out=dbg["dbg_kdr"], in_=kdr),
                  dma=True, reads=["kdrd"], writes=["dbg_kdr"])
            def _dkscr(e):
                return e.dma_start(out=dbg["dbg_kscr"], in_=kscr[0:2, 0:4])
            P.add("gpsimd", _dkscr, dma=True, reads=["kscrd"], writes=["dbg_kscr"])
        for i in range(NB):
            build_block_b(i)

        P.resolve()

        import contextlib
        with contextlib.ExitStack() as es:
            sems = {}
            for ki, key in enumerate(P.sem_keys):
                sems[key] = es.enter_context(nc.semaphore(f"sm{ki}"))
            block = es.enter_context(nc.Block())

            @block.sync
            def _(e):
                P.emit("sync", e, sems)

            @block.scalar
            def _(e):
                P.emit("scalar", e, sems)

            @block.vector
            def _(e):
                P.emit("vector", e, sems)

            @block.tensor
            def _(e):
                P.emit("tensor", e, sems)

            @block.gpsimd
            def _(e):
                P.emit("gpsimd", e, sems)
                e.wait_ge(sems[("r", "outd")], P.dma_final["outd"])

    return nc, sbuf_left, len(P.ops)


_NC_CACHE = None


def kernel(L0_attn_x, L0_attn_qkv_weights, L0_ffn_router_W, L0_ffn_router_b,
           L0_ffn_W1, L0_ffn_b1, L0_ffn_W2, L0_ffn_b2):
    global _NC_CACHE
    x = np.asarray(L0_attn_x, dtype=np.float32)
    qkvw = np.asarray(L0_attn_qkv_weights, dtype=np.float32)
    rW = np.asarray(L0_ffn_router_W, dtype=np.float32)
    rb = np.asarray(L0_ffn_router_b, dtype=np.float32)
    W1 = np.asarray(L0_ffn_W1, dtype=np.float32)
    b1 = np.asarray(L0_ffn_b1, dtype=np.float32)
    W2 = np.asarray(L0_ffn_W2, dtype=np.float32)
    b2 = np.asarray(L0_ffn_b2, dtype=np.float32)

    if _NC_CACHE is None:
        _NC_CACHE = build_nc()
    nc, _, _ = _NC_CACHE

    qkvw_l = np.ascontiguousarray(qkvw.reshape(DC, 128, 3 * D))
    rwT_l = np.ascontiguousarray(rW.reshape(DC, 128, NE).transpose(1, 0, 2)).reshape(128, DC * NE)
    rb_l = np.ascontiguousarray(np.broadcast_to(rb[None, :], (128, NE)))
    b1T_l = np.ascontiguousarray(b1.reshape(NE, FCN, 128).transpose(2, 0, 1)).reshape(128, NE * FCN)
    b2T_l = np.ascontiguousarray(b2.reshape(NE, DC, 128).transpose(2, 0, 1)).reshape(128, NE * DC)
    w1_l = np.ascontiguousarray(W1.reshape(NE, DC, 128, DFF))
    w2_l = np.ascontiguousarray(W2.reshape(NE, FCN, 128, D))

    in_maps = []
    for b in range(8):
        x_fm = np.ascontiguousarray(x[b].T).reshape(DC, 128, NB, T)
        in_maps.append({
            "x_fm": x_fm, "qkvw": qkvw_l, "rwT": rwT_l, "rb_bc": rb_l,
            "b1T": b1T_l, "b2T": b2T_l, "w1": w1_l, "w2": w2_l,
        })
    res = run_bass_kernel_spmd(nc, in_maps, core_ids=list(range(8)))
    out = np.empty((8, S, D), np.float32)
    for b in range(8):
        o = res.results[b]["out_fm"]
        out[b] = o.transpose(2, 3, 0, 1).reshape(S, D)
    return out


# version 04
# speedup vs baseline: 1.0398x; 1.0220x over previous
import numpy as np
import ml_dtypes
import concourse.bass as bass
import concourse.mybir as mybir
from concourse.bass_utils import run_bass_kernel_spmd

F32 = mybir.dt.float32
BF16 = mybir.dt.bfloat16
AF = mybir.ActivationFunctionType
ALU = mybir.AluOpType
AX = mybir.AxisListType

D = 1024
S = 4096
NB = 8
T = 512
DC = 8
FCN = 32
NE = 8
NH = 16
DH = 64
NSUB = 4
DFF = 4096
EPS = 1e-5
EXP_SHIFT = -20.0

ENGINES = ("tensor", "vector", "scalar", "gpsimd", "sync")
DEBUG_DUMPS = False


class Prog:

    def __init__(self):
        self.ops = []
        self.last_writer = {}
        self.readers = {}

    def add(self, engine, fn, reads=(), writes=(), dma=False):
        idx = len(self.ops)
        deps = set()
        for r in reads:
            w = self.last_writer.get(r)
            if w is not None:
                deps.add(w)
        for w_ in writes:
            w = self.last_writer.get(w_)
            if w is not None:
                deps.add(w)
            for rd in self.readers.get(w_, ()):
                deps.add(rd)
        deps.discard(idx)
        wres = writes[0] if dma else None
        if dma:
            assert len(writes) == 1, "dma ops must write exactly one resource"
        self.ops.append(dict(engine=engine, fn=fn, dma=dma, deps=deps,
                             need_inc=False, idx=idx, wres=wres))
        for w_ in writes:
            self.last_writer[w_] = idx
            self.readers[w_] = []
        for r in reads:
            self.readers.setdefault(r, []).append(idx)
        return idx

    def resolve(self):
        for op in self.ops:
            kept = []
            for d in sorted(op["deps"]):
                p = self.ops[d]
                if p["engine"] == "tensor" and op["engine"] == "tensor" \
                        and not p["dma"] and not op["dma"]:
                    continue
                kept.append(d)
                if not p["dma"]:
                    p["need_inc"] = True
            op["edges"] = kept
        cc = {e: 0 for e in ENGINES}
        rc = {}
        for op in self.ops:
            e = op["engine"]
            if op["dma"]:
                r = op["wres"]
                rc[r] = rc.get(r, 0) + 1
                op["post"] = ("r", r, rc[r] * 16)
            elif op["need_inc"]:
                cc[e] += 1
                op["post"] = ("c", e, cc[e])
            else:
                op["post"] = None
        self.dma_final = {r: v * 16 for r, v in rc.items()}
        self.sem_keys = [("c", e) for e in ENGINES if e != "sync"] +                         [("r", r) for r in rc]
        know = {e: {} for e in ENGINES}
        for op in self.ops:
            k = know[op["engine"]]
            waits = {}
            for d in op["edges"]:
                p = self.ops[d]
                assert p["post"] is not None, (op["idx"], d)
                kind, pe, val = p["post"]
                sem = (kind, pe)
                if k.get(sem, 0) >= val:
                    continue
                if waits.get(sem, 0) < val:
                    waits[sem] = val
            for sem, val in waits.items():
                k[sem] = val
            op["waits"] = sorted(waits.items())
            if op["engine"] == "tensor" and op["post"] is not None and op["post"][0] == "c":
                k[("c", "tensor")] = op["post"][2]

    def emit(self, engine_name, eng, sems):
        for op in self.ops:
            if op["engine"] != engine_name:
                continue
            for (sem, val) in op["waits"]:
                eng.wait_ge(sems[sem], val)
            inst = op["fn"](eng)
            if op["dma"]:
                inst.then_inc(sems[("r", op["wres"])], 16)
            elif op["need_inc"]:
                if inst is None:
                    eng.nop().then_inc(sems[("c", engine_name)], 1)
                else:
                    inst.then_inc(sems[("c", engine_name)], 1)


def build_nc():
    nc = bass.Bass("TRN2", target_bir_lowering=False, debug=False)

    xin = nc.dram_tensor("x_fm", [DC, 128, NB, T], F32, kind="ExternalInput").ap()
    qkvw = nc.dram_tensor("qkvw", [DC, 128, 3 * D], F32, kind="ExternalInput").ap()
    rwT = nc.dram_tensor("rwT", [128, DC * NE], F32, kind="ExternalInput").ap()
    rbbc_in = nc.dram_tensor("rb_bc", [128, NE], F32, kind="ExternalInput").ap()
    b1T_in = nc.dram_tensor("b1T", [128, NE * FCN], F32, kind="ExternalInput").ap()
    b2T_in = nc.dram_tensor("b2T", [128, NE * DC], F32, kind="ExternalInput").ap()
    w1in = nc.dram_tensor("w1", [NE, DC, 128, DFF], F32, kind="ExternalInput").ap()
    w2in = nc.dram_tensor("w2", [NE, FCN, 128, D], F32, kind="ExternalInput").ap()
    outap = nc.dram_tensor("out_fm", [DC, 128, NB, T], F32, kind="ExternalOutput").ap()
    dbg = {}
    if DEBUG_DUMPS:
        for nm, shape, dt in [("dbg_normed1", [128, DC, T], BF16),
                              ("dbg_x2", [128, DC, T], F32),
                              ("dbg_gbc", [128, NE, T], BF16),
                              ("dbg_q", [128, DC, T], BF16),
                              ("dbg_k", [128, DC, T], BF16),
                              ("dbg_v", [128, NSUB, NH * 65], BF16),
                              ("dbg_x3", [128, DC, T], F32),
                              ("dbg_h", [128, FCN * T], BF16)]:
            dbg[nm] = nc.dram_tensor(nm, shape, dt, kind="ExternalOutput").ap()
        dbg["dbg_kdr"] = nc.dram_tensor("dbg_kdr", [DC, 128, S], BF16,
                                        kind="ExternalOutput").ap()
        dbg["dbg_kscr"] = nc.dram_tensor("dbg_kscr", [2, 4, 128, 128], BF16,
                                         kind="ExternalOutput").ap()

    w1b = [nc.dram_tensor(f"w1b{e}", [FCN, 128, DC, 128], BF16, kind="Internal").ap()
           for e in range(NE)]
    w2b = [nc.dram_tensor(f"w2b{e}", [FCN, 128, D], BF16, kind="Internal").ap()
           for e in range(NE)]
    qkvb = nc.dram_tensor("qkvb", [DC, 128, 3 * D], BF16, kind="Internal").ap()
    gsc = nc.dram_tensor("gsc", [1, NE, T], BF16, kind="Internal").ap()
    kdr = nc.dram_tensor("kdr", [DC, 128, S], BF16, kind="Internal").ap()
    vdr = nc.dram_tensor("vdr", [S, NH * 65], BF16, kind="Internal").ap()
    x2d = nc.dram_tensor("x2d", [DC, 128, NB, T], F32, kind="Internal").ap()
    qdr = nc.dram_tensor("qdr", [DC, 128, NB, T], BF16, kind="Internal").ap()
    kscr = nc.dram_tensor("kscr", [NH // 2, FCN, 128, 128], BF16, kind="Internal").ap()

    sb = {}

    def sbuf(name, shape, dt):
        sb[name] = nc.alloc_sbuf_tensor(name, shape, dt)
        return sb[name]

    rwt = sbuf("rwt", [128, DC * NE], BF16)
    rbbc = sbuf("rbbc", [128, NE], F32)
    b1t = sbuf("b1t", [128, NE * FCN], F32)
    b2t = sbuf("b2t", [128, NE * DC], F32)
    iden = sbuf("iden", [128, 128], F32)
    iotap = sbuf("iotap", [128, 1], F32)
    iota8 = sbuf("iota8", [128, 8], F32)
    iom9 = sbuf("iom9", [128, 8], F32)
    onesc = sbuf("onesc", [128, 1], F32)
    ones1 = sbuf("ones1", [1, 128], F32)
    masks = sbuf("masks", [128, 4, T], BF16)
    iotaq = sbuf("iotaq", [128, T], F32)

    xt = sbuf("xt", [128, 2, DC, T], F32)
    acc = sbuf("acc", [128, DC, T], F32)
    normed = sbuf("normed", [128, DC, T], BF16)
    hbuf = sbuf("hbuf", [128, FCN * T], BF16)
    sqv = sb["hbuf"].bitcast(F32)
    gbc = sbuf("gbc", [128, NE, T], BF16)
    qbf = sbuf("qbf", [128, DC, T], BF16)
    kst = sbuf("kst", [128, DC, T], BF16)
    vst = sbuf("vst", [128, NSUB, NH * 65], BF16)

    w1r = [sbuf(f"w1r{s}", [128, DC, 128], BF16) for s in range(4)]
    w2r = [sbuf(f"w2r{s}", [128, DC, 128], BF16) for s in range(4)]
    qkr = [sbuf(f"qkr{s}", [128, 1024], BF16) for s in range(2)]
    ksrng = [sbuf(f"ksrng{s}", [128, 128], BF16) for s in range(6)]
    katile = [sbuf(f"katile{s}", [128, 128], BF16) for s in range(2)]
    kstg = [sbuf(f"kstg{s}", [128, 128], BF16) for s in range(2)]
    iden_bf = sbuf("iden_bf", [128, 128], BF16)
    vrng = [sbuf(f"vrng{s}", [128, NH * 65], BF16) for s in range(4)]
    erng = [sbuf(f"erng{s}", [128, T], BF16) for s in range(4)]
    cstg = [sbuf(f"cstg{s}", [128, 2048], BF16) for s in range(2)]

    rows = sbuf("rows", [1, 7 * T], F32)
    grow8 = sbuf("grow8", [1, NE * T], BF16)
    eps1 = sbuf("eps1", [1, 1], F32)
    shf = sbuf("shf", [128, 1], F32)
    ones1b = sbuf("ones1b", [1, 128], BF16)
    a_sb = sbuf("a_sb", [128, T], F32)
    b_sb = sbuf("b_sb", [128, T], F32)
    rsbc = sbuf("rsbc", [128, T], F32)
    tmp1 = sbuf("tmp1", [128, T], F32)
    tmp2 = sbuf("tmp2", [128, T], F32)
    lgt = sbuf("lgt", [128, 8], F32)
    lmax = sbuf("lmax", [128, 1], F32)
    negm = sbuf("negm", [128, 1], F32)
    expv = sbuf("expv", [128, 8], F32)
    sume = sbuf("sume", [128, 1], F32)
    gval = sbuf("gval", [128, 1], F32)
    meq = sbuf("meq", [128, 8], F32)
    selv = sbuf("selv", [128, 8], F32)
    emin = sbuf("emin", [128, 1], F32)
    g8 = sbuf("g8", [128, 8], F32)
    grs = sbuf("grs", [8, T], F32)

    sbuf_left = nc.sbuf_bytes_remaining
    assert sbuf_left >= 0, sbuf_left

    R_MU, R_MSQ, R_VAR, R_LNV, R_A, R_B, R_RS = range(7)

    def row(i):
        return sb["rows"][0:1, i * T:(i + 1) * T]

    P = Prog()

    with (
        nc.psum_tensor("pb0", [128, T], F32) as pb0,
        nc.psum_tensor("pb1", [128, T], F32) as pb1,
        nc.psum_tensor("pb2", [128, T], F32) as pb2,
        nc.psum_tensor("pb3", [128, T], F32) as pb3,
        nc.psum_tensor("pb4", [128, T], F32) as pb4,
        nc.psum_tensor("pb5", [128, T], F32) as pb5,
        nc.psum_tensor("pb6", [128, T], F32) as pb6,
        nc.psum_tensor("pb7", [128, T], F32) as pb7,
    ):
        pb = [pb0, pb1, pb2, pb3, pb4, pb5, pb6, pb7]
        pname = [f"pb{i}" for i in range(8)]

        def prologue():
            P.add("gpsimd", lambda e: e.iota(iotap[:], [[0, 1]], channel_multiplier=1,
                                             allow_small_or_imprecise_dtypes=True),
                  writes=["iotap"])
            P.add("gpsimd", lambda e: e.iota(iota8[:], [[1, 8]], channel_multiplier=0,
                                             allow_small_or_imprecise_dtypes=True),
                  writes=["iota8"])
            P.add("gpsimd", lambda e: e.iota(iotaq[:], [[1, T]], channel_multiplier=0,
                                             allow_small_or_imprecise_dtypes=True),
                  writes=["iotaq"])
            P.add("vector", lambda e: e.tensor_scalar(iom9[:], iota8[:], -9.0, None,
                                                      ALU.add),
                  reads=["iota8"], writes=["iom9"])
            P.add("vector", lambda e: e.memset(onesc[:], 1.0), writes=["onesc"])
            P.add("vector", lambda e: e.memset(ones1[:], 1.0), writes=["ones1"])
            P.add("vector", lambda e: e.memset(ones1b[:], 1.0), writes=["ones1b"])
            P.add("vector", lambda e: e.memset(eps1[:], EPS), writes=["eps1"])
            P.add("vector", lambda e: e.memset(shf[:], EXP_SHIFT), writes=["shf"])
            P.add("vector", lambda e: e.memset(vst[:], 1.0), writes=["vst"])
            P.add("gpsimd", lambda e: e.iota(iden[:], [[1, 128]], channel_multiplier=0,
                                             allow_small_or_imprecise_dtypes=True),
                  writes=["iden"])
            P.add("vector", lambda e: e.tensor_scalar(iden[:], iden[:], iotap[:], None,
                                                      ALU.is_equal),
                  reads=["iotap"], writes=["iden"])
            P.add("vector", lambda e: e.tensor_copy(iden_bf[:], iden[:]),
                  reads=["iden"], writes=["iden_bf"])
            for m in range(4):
                def mk(e, m=m):
                    return e.tensor_scalar(negm[:], iotap[:], float(128 * m), None, ALU.add)
                P.add("vector", mk, reads=["iotap"], writes=["negm"])
                def mk2(e, m=m):
                    return e.tensor_scalar(masks[:, m, :], iotaq[:], negm[:], None, ALU.is_ge)
                P.add("vector", mk2, reads=["iotaq", "negm"], writes=["masks"])
            P.add("gpsimd", lambda e: e.dma_start(out=rwt[:], in_=rwT), dma=True,
                  writes=["rwt"])
            P.add("sync", lambda e: e.dma_start(out=rbbc[:], in_=rbbc_in), dma=True,
                  writes=["rbbc"])
            P.add("sync", lambda e: e.dma_start(out=b1t[:], in_=b1T_in), dma=True,
                  writes=["b1t"])
            P.add("sync", lambda e: e.dma_start(out=b2t[:], in_=b2T_in), dma=True,
                  writes=["b2t"])
            slc = [0]
            for e_ in range(NE):
                for fs2 in range(FCN // 2):
                    sl = slc[0] % 2
                    slc[0] += 1
                    def ld(eng, e_=e_, fs2=fs2, sl=sl):
                        src = w1in[e_, :, :, fs2 * 256:(fs2 + 1) * 256].transpose(
                            [1, 0, 2])
                        return eng.dma_start(out=cstg[sl][:].rearrange(
                            "p (dc f) -> p dc f", dc=DC), in_=src)
                    P.add("gpsimd", ld, dma=True, writes=[f"cstg{sl}"])
                    def st(eng, e_=e_, fs2=fs2, sl=sl):
                        dst = w1b[e_][2 * fs2:2 * fs2 + 2].transpose([1, 2, 0, 3])
                        return eng.dma_start(out=dst, in_=cstg[sl][:].rearrange(
                            "p (dc fcp f) -> p dc fcp f", dc=DC, fcp=2))
                    P.add("sync", st, dma=True, reads=[f"cstg{sl}"], writes=[f"w1bd{e_}"])
                for fs2 in range(FCN // 2):
                    sl = slc[0] % 2
                    slc[0] += 1
                    def ld2(eng, e_=e_, fs2=fs2, sl=sl):
                        src = w2in[e_, 2 * fs2:2 * fs2 + 2, :, :].transpose([1, 0, 2])
                        return eng.dma_start(out=cstg[sl][:].rearrange(
                            "p (fcp d) -> p fcp d", fcp=2), in_=src)
                    P.add("gpsimd", ld2, dma=True, writes=[f"cstg{sl}"])
                    def st2(eng, e_=e_, fs2=fs2, sl=sl):
                        dst = w2b[e_][2 * fs2:2 * fs2 + 2].transpose([1, 0, 2])
                        return eng.dma_start(out=dst, in_=cstg[sl][:].rearrange(
                            "p (fcp d) -> p fcp d", fcp=2))
                    P.add("sync", st2, dma=True, reads=[f"cstg{sl}"], writes=[f"w2bd{e_}"])
            for dc in range(DC):
                for qs in range(3):
                    sl = slc[0] % 2
                    slc[0] += 1
                    def ld3(eng, dc=dc, qs=qs, sl=sl):
                        return eng.dma_start(out=cstg[sl][:, 0:1024],
                                             in_=qkvw[dc][:, qs * 1024:(qs + 1) * 1024])
                    P.add("gpsimd", ld3, dma=True, writes=[f"cstg{sl}"])
                    def st3(eng, dc=dc, qs=qs, sl=sl):
                        return eng.dma_start(out=qkvb[dc][:, qs * 1024:(qs + 1) * 1024],
                                             in_=cstg[sl][:, 0:1024])
                    P.add("sync", st3, dma=True, reads=[f"cstg{sl}"], writes=["qkvbd"])

        def layernorm(src_view, src_res):
            for dc in range(DC):
                def sq(e, dc=dc):
                    return e.tensor_tensor(sqv[:, dc * T:(dc + 1) * T], src_view(dc),
                                           src_view(dc), ALU.mult)
                P.add("vector", sq, reads=[src_res], writes=["hbuf"])
            for dc in range(DC):
                def mm(e, dc=dc):
                    return e.matmul(pb6[0:1, :], onesc[:], src_view(dc),
                                    start=(dc == 0), stop=(dc == DC - 1))
                P.add("tensor", mm, reads=[src_res, "onesc"], writes=["pb6"])
            for dc in range(DC):
                def mm2(e, dc=dc):
                    return e.matmul(pb7[0:1, :], onesc[:], sqv[:, dc * T:(dc + 1) * T],
                                    start=(dc == 0), stop=(dc == DC - 1))
                P.add("tensor", mm2, reads=["hbuf", "onesc"], writes=["pb7"])
            P.add("vector", lambda e: e.tensor_scalar(row(R_MU), pb6[0:1, :], 1.0 / D, None,
                                                      ALU.mult),
                  reads=["pb6"], writes=["rows"])
            P.add("vector", lambda e: e.tensor_scalar(row(R_MSQ), pb7[0:1, :], 1.0 / D, None,
                                                      ALU.mult),
                  reads=["pb7"], writes=["rows"])
            P.add("vector", lambda e: e.scalar_tensor_tensor(row(R_VAR), row(R_MU), -1.0,
                                                             row(R_MU), ALU.mult, ALU.mult),
                  reads=["rows"], writes=["rows"])
            P.add("vector", lambda e: e.tensor_tensor(row(R_VAR), row(R_VAR), row(R_MSQ),
                                                      ALU.add),
                  reads=["rows"], writes=["rows"])
            P.add("scalar", lambda e: e.activation(row(R_LNV), row(R_VAR), AF.Ln,
                                                   bias=eps1[:]),
                  reads=["rows", "eps1"], writes=["rows"])
            P.add("scalar", lambda e: e.activation(row(R_A), row(R_LNV), AF.Exp, scale=-0.5),
                  reads=["rows"], writes=["rows"])
            P.add("vector", lambda e: e.scalar_tensor_tensor(row(R_B), row(R_MU), -1.0,
                                                             row(R_A), ALU.mult, ALU.mult),
                  reads=["rows"], writes=["rows"])
            P.add("tensor", lambda e: e.matmul(pb6[:], ones1[:], row(R_A), start=True,
                                               stop=True),
                  reads=["rows", "ones1"], writes=["pb6"])
            P.add("tensor", lambda e: e.matmul(pb7[:], ones1[:], row(R_B), start=True,
                                               stop=True),
                  reads=["rows", "ones1"], writes=["pb7"])
            P.add("vector", lambda e: e.tensor_copy(a_sb[:], pb6[:]),
                  reads=["pb6"], writes=["a_sb"])
            P.add("vector", lambda e: e.tensor_copy(b_sb[:], pb7[:]),
                  reads=["pb7"], writes=["b_sb"])
            for dc in range(DC):
                def n1(e, dc=dc):
                    return e.tensor_tensor(tmp1[:], src_view(dc), a_sb[:], ALU.mult)
                P.add("vector", n1, reads=[src_res, "a_sb"], writes=["tmp1"])
                def n2(e, dc=dc):
                    return e.tensor_tensor(normed[:, dc, :], tmp1[:], b_sb[:], ALU.add)
                P.add("vector", n2, reads=["tmp1", "b_sb"], writes=["normed"])

        def router():
            tbanks = [2, 3, 4, 5]
            for sub in range(NSUB):
                for dc in range(DC):
                    def mm(e, sub=sub, dc=dc):
                        return e.matmul(pb6[:, 0:8],
                                        normed[:, dc, sub * 128:(sub + 1) * 128],
                                        rwt[:, dc * 8:(dc + 1) * 8],
                                        start=(dc == 0), stop=(dc == DC - 1))
                    P.add("tensor", mm, reads=["normed", "rwt"], writes=["pb6"])
                P.add("vector", lambda e: e.tensor_tensor(lgt[:], pb6[:, 0:8], rbbc[:],
                                                          ALU.add),
                      reads=["pb6", "rbbc"], writes=["lgt"])
                P.add("vector", lambda e: e.tensor_reduce(lmax[:], lgt[:], AX.X, ALU.max),
                      reads=["lgt"], writes=["lmax"])
                P.add("vector", lambda e: e.tensor_scalar(negm[:], lmax[:], -1.0, None,
                                                          ALU.mult),
                      reads=["lmax"], writes=["negm"])
                P.add("scalar", lambda e: e.activation(expv[:], lgt[:], AF.Exp,
                                                       bias=negm[:]),
                      reads=["lgt", "negm"], writes=["expv"])
                P.add("vector", lambda e: e.tensor_reduce(sume[:], expv[:], AX.X, ALU.add),
                      reads=["expv"], writes=["sume"])
                P.add("vector", lambda e: e.reciprocal(gval[:], sume[:]),
                      reads=["sume"], writes=["gval"])
                P.add("vector", lambda e: e.tensor_scalar(gval[:], gval[:], 0.5, None,
                                                          ALU.mult),
                      reads=["gval"], writes=["gval"])
                P.add("vector", lambda e: e.tensor_scalar(meq[:], lgt[:], lmax[:], None,
                                                          ALU.is_equal),
                      reads=["lgt", "lmax"], writes=["meq"])
                P.add("vector", lambda e: e.tensor_tensor(selv[:], meq[:], iom9[:],
                                                          ALU.mult),
                      reads=["meq", "iom9"], writes=["selv"])
                P.add("vector", lambda e: e.tensor_scalar(selv[:], selv[:], 9.0, None,
                                                          ALU.add),
                      reads=["selv"], writes=["selv"])
                P.add("vector", lambda e: e.tensor_reduce(emin[:], selv[:], AX.X, ALU.min),
                      reads=["selv"], writes=["emin"])
                P.add("vector", lambda e: e.tensor_scalar(meq[:], iota8[:], emin[:], None,
                                                          ALU.is_equal),
                      reads=["iota8", "emin"], writes=["meq"])
                P.add("vector", lambda e: e.tensor_scalar(g8[:], meq[:], gval[:], None,
                                                          ALU.mult),
                      reads=["meq", "gval"], writes=["g8"])
                bk = tbanks[sub]
                def tr(e, sub=sub, bk=bk):
                    return e.transpose(pb[bk][0:8, 0:128], g8[:], iden[:])
                P.add("tensor", tr, reads=["g8", "iden"], writes=[pname[bk]])
                def ev(e, sub=sub, bk=bk):
                    return e.tensor_copy(grs[:, sub * 128:(sub + 1) * 128],
                                         pb[bk][0:8, 0:128])
                P.add("vector", ev, reads=[pname[bk]], writes=["grs"])
            P.add("gpsimd", lambda e: e.dma_start(out=gsc[0], in_=grs[:]), dma=True,
                  reads=["grs"], writes=["gscd"])
            def mvrow(e):
                dst = grow8[0:1, :].rearrange("o (e t) -> o e t", e=8)
                return e.dma_start(out=dst, in_=gsc)
            P.add("sync", mvrow, dma=True, reads=["gscd"], writes=["grow8"])
            for e_ in range(NE):
                bk = 6 if e_ % 2 == 0 else 7
                def mm(e, e_=e_, bk=bk):
                    return e.matmul(pb[bk][:], ones1b[:],
                                    grow8[0:1, e_ * T:(e_ + 1) * T], start=True, stop=True)
                P.add("tensor", mm, reads=["grow8", "ones1b"], writes=[pname[bk]])
                def ev2(e, e_=e_, bk=bk):
                    return e.tensor_copy(gbc[:, e_, :], pb[bk][:])
                P.add("vector", ev2, reads=[pname[bk]], writes=["gbc"])

        def ffn(iblk):
            def ld_w1(e_, fc):
                sl = fc % 4
                def ld(eng, e_=e_, fc=fc, sl=sl):
                    return eng.dma_start(out=w1r[sl][:], in_=w1b[e_][fc])
                P.add("sync", ld, dma=True, reads=[f"w1bd{e_}"], writes=[f"w1r{sl}"])

            def ld_w2(e_, fc):
                sl = fc % 4
                def ld2(eng, e_=e_, fc=fc, sl=sl):
                    return eng.dma_start(out=w2r[sl][:].rearrange(
                        "p dc f -> p (dc f)"), in_=w2b[e_][fc])
                P.add("sync", ld2, dma=True, reads=[f"w2bd{e_}"], writes=[f"w2r{sl}"])

            for e_ in range(NE):
                for fc in range(4):
                    ld_w1(e_, fc)
                for fc in range(FCN):
                    sl = fc % 4
                    hb = 6 if fc % 2 == 0 else 7
                    for dc in range(DC):
                        def mm(e, fc=fc, dc=dc, sl=sl, hb=hb):
                            return e.matmul(pb[hb][:], w1r[sl][:, dc, :], normed[:, dc, :],
                                            start=(dc == 0), stop=(dc == DC - 1))
                        P.add("tensor", mm, reads=[f"w1r{sl}", "normed"], writes=[pname[hb]])
                    if fc + 4 < FCN:
                        ld_w1(e_, fc + 4)
                    def relu(e, e_=e_, fc=fc, hb=hb):
                        col = e_ * FCN + fc
                        return e.activation(hbuf[:, fc * T:(fc + 1) * T], pb[hb][:],
                                            AF.Relu, bias=b1t[:, col:col + 1])
                    P.add("scalar", relu, reads=[pname[hb], "b1t"], writes=["hbuf"])
                for fc in range(4):
                    ld_w2(e_, fc)
                for half in range(2):
                    for fc in range(FCN):
                        sl = fc % 4
                        for dci in range(4):
                            dc = half * 4 + dci
                            yb = 2 + dci
                            def mm2(e, fc=fc, dc=dc, sl=sl, yb=yb):
                                return e.matmul(pb[yb][:], w2r[sl][:, dc, :],
                                                hbuf[:, fc * T:(fc + 1) * T],
                                                start=(fc == 0), stop=(fc == FCN - 1))
                            P.add("tensor", mm2, reads=[f"w2r{sl}", "hbuf"],
                                  writes=[pname[yb]])
                        if fc + 4 < FCN:
                            ld_w2(e_, fc + 4)
                    if half == 0:
                        for fc in range(4):
                            ld_w2(e_, fc)
                    for dci in range(4):
                        dc = half * 4 + dci
                        yb = 2 + dci
                        def ev(e, e_=e_, dc=dc, yb=yb):
                            col = e_ * DC + dc
                            return e.scalar_tensor_tensor(tmp2[:], pb[yb][:],
                                                          b2t[:, col:col + 1],
                                                          gbc[:, e_, :], ALU.add, ALU.mult)
                        P.add("vector", ev, reads=[pname[yb], "b2t", "gbc"], writes=["tmp2"])
                        def ac(e, dc=dc):
                            return e.tensor_tensor(acc[:, dc, :], acc[:, dc, :], tmp2[:],
                                                   ALU.add)
                        P.add("vector", ac, reads=["tmp2", "acc"], writes=["acc"])

        def qkv(iblk):
            ocgroups = [list(range(0, 6)), list(range(6, 12)), list(range(12, 16))]
            for grp in ocgroups:
                base = grp[0] * 128
                width = len(grp) * 128
                for dc in range(DC):
                    sl = dc % 2
                    def ld(eng, dc=dc, sl=sl, base=base, width=width):
                        return eng.dma_start(out=qkr[sl][:, 0:width],
                                             in_=qkvb[dc][:, base:base + width])
                    P.add("sync", ld, dma=True, reads=["qkvbd"], writes=[f"qkr{sl}"])
                    for k, oc in enumerate(grp):
                        def mm(e, dc=dc, sl=sl, k=k):
                            return e.matmul(pb[k][:], qkr[sl][:, k * 128:(k + 1) * 128],
                                            normed[:, dc, :], start=(dc == 0),
                                            stop=(dc == DC - 1))
                        P.add("tensor", mm, reads=[f"qkr{sl}", "normed"], writes=[pname[k]])
                for k, oc in enumerate(grp):
                    if oc < 8:
                        def ev(e, k=k, oc=oc):
                            return e.tensor_copy(qbf[:, oc, :], pb[k][:])
                        P.add("vector", ev, reads=[pname[k]], writes=["qbf"])
                    else:
                        def ev(e, k=k, oc=oc):
                            return e.tensor_copy(kst[:, oc - 8, :], pb[k][:])
                        P.add("vector", ev, reads=[pname[k]], writes=["kst"])
            for grp in ([(0, 0), (0, 1), (1, 0), (1, 1)], [(2, 0), (2, 1), (3, 0), (3, 1)]):
                for dc in range(DC):
                    sl = dc % 2
                    def ld2(eng, dc=dc, sl=sl):
                        return eng.dma_start(out=qkr[sl][:], in_=qkvb[dc][:, 2 * D:3 * D])
                    P.add("sync", ld2, dma=True, reads=["qkvbd"], writes=[f"qkr{sl}"])
                    for k, (sub, vh) in enumerate(grp):
                        def mm(e, dc=dc, sl=sl, sub=sub, vh=vh, k=k):
                            return e.matmul(pb[2 + k][:],
                                            normed[:, dc, sub * 128:(sub + 1) * 128],
                                            qkr[sl][:, vh * 512:(vh + 1) * 512],
                                            start=(dc == 0), stop=(dc == DC - 1))
                        P.add("tensor", mm, reads=[f"qkr{sl}", "normed"],
                              writes=[pname[2 + k]])
                for k, (sub, vh) in enumerate(grp):
                    def ev2(e, k=k, sub=sub, vh=vh):
                        dst = vst[:, sub, :].rearrange("p (h c) -> p h c", c=65)
                        dst = dst[:, 8 * vh:8 * vh + 8, 0:64]
                        src = pb[2 + k][:].rearrange("p (h c) -> p h c", c=64)
                        return e.tensor_copy(dst, src)
                    P.add("vector", ev2, reads=[pname[2 + k]], writes=["vst"])
            def stk(e, iblk=iblk):
                dst = kdr[:, :, iblk * T:(iblk + 1) * T].transpose([1, 0, 2])
                return e.dma_start(out=dst, in_=kst[:].rearrange("p dc t -> p dc t"))
            P.add("gpsimd", stk, dma=True, reads=["kst"], writes=["kdrd"])
            def stv(e, iblk=iblk):
                dst = vdr[iblk * T:(iblk + 1) * T, :].rearrange("(sub p) c -> p sub c", p=128)
                return e.dma_start(out=dst, in_=vst[:])
            P.add("gpsimd", stv, dma=True, reads=["vst"], writes=["vdrd"])

        def kscr_pass():
            cnt = [0]
            for hp in range(NH // 2):
                for jc in range(FCN):
                    sl = cnt[0] % 2
                    hb = 6 if cnt[0] % 2 == 0 else 7
                    cnt[0] += 1
                    def ld(e, hp=hp, jc=jc, sl=sl):
                        return e.dma_start(
                            out=katile[sl][:],
                            in_=kdr[hp, :, :].rearrange(
                                "(hh pr) (m c) -> hh pr m c", hh=2, c=64)[
                                :, 2 * jc:2 * jc + 2, :, :].transpose([1, 2, 0, 3]))
                    P.add("scalar", ld, dma=True, reads=["kdrd"], writes=[f"katile{sl}"])
                    def tr(e, sl=sl, hb=hb):
                        out_bf = pb[hb].bitcast(BF16)[0:128, 0:128]
                        return e.transpose(out_bf, katile[sl][:], iden_bf[:])
                    P.add("tensor", tr, reads=[f"katile{sl}", "iden_bf"], writes=[pname[hb]])
                    def ev(e, sl=sl, hb=hb):
                        return e.tensor_copy(kstg[sl][:], pb[hb].bitcast(BF16)[0:128, 0:128])
                    P.add("vector", ev, reads=[pname[hb]], writes=[f"kstg{sl}"])
                    def st(e, hp=hp, jc=jc, sl=sl):
                        return e.dma_start(out=kscr[hp, jc], in_=kstg[sl][:])
                    P.add("gpsimd", st, dma=True, reads=[f"kstg{sl}"], writes=["kscrd"])

        def attention(iblk):
            jlast = 4 * iblk + 3
            passes = [list(range(0, 6)), list(range(6, 12)), list(range(12, 16))]
            for heads in passes:
                npair = len(heads) // 2
                base = heads[0]
                kcnt = [0]
                kslot = {}
                def ldkv(j, pairs_hp):
                    sl = j % 4
                    def f2(e, j=j, sl=sl):
                        return e.dma_start(out=vrng[sl][:], in_=vdr[j * 128:(j + 1) * 128, :])
                    P.add("scalar", f2, dma=True, reads=["vdrd"], writes=[f"vrng{sl}"])
                    for hp in pairs_hp:
                        ks = kcnt[0] % 6
                        kcnt[0] += 1
                        kslot[(j, hp)] = ks
                        def f3(e, j=j, hp=hp, ks=ks):
                            return e.dma_start(out=ksrng[ks][:], in_=kscr[hp, j])
                        P.add("scalar", f3, dma=True, reads=["kscrd"],
                              writes=[f"ksrng{ks}"])
                pairs_hp = [(base // 2) + pi for pi in range(npair)]
                ldkv(0, pairs_hp)
                esl = 0
                for j in range(jlast + 1):
                    if j + 1 <= jlast:
                        ldkv(j + 1, pairs_hp)
                    sl = j % 4
                    m = j - 4 * iblk
                    for pi in range(npair):
                        c = (base // 2) + pi
                        hA, hB = 2 * c, 2 * c + 1
                        eA, eB = esl % 4, (esl + 1) % 4
                        esl += 2
                        ks = kslot[(j, c)]
                        def mmA(e, c=c, ks=ks):
                            return e.matmul(pb6[:], ksrng[ks][0:64, :],
                                            qbf[0:64, c, :], start=True, stop=True)
                        P.add("tensor", mmA, reads=[f"ksrng{ks}", "qbf"], writes=["pb6"])
                        def mmB(e, c=c, ks=ks):
                            return e.matmul(pb7[:], ksrng[ks][64:128, :],
                                            qbf[64:128, c, :], start=True, stop=True)
                        P.add("tensor", mmB, reads=[f"ksrng{ks}", "qbf"], writes=["pb7"])
                        def exA(e, eA=eA):
                            return e.activation(erng[eA][:], pb6[:], AF.Exp, scale=0.125,
                                                bias=shf[:])
                        P.add("scalar", exA, reads=["pb6", "shf"], writes=[f"erng{eA}"])
                        def exB(e, eB=eB):
                            return e.activation(erng[eB][:], pb7[:], AF.Exp, scale=0.125,
                                                bias=shf[:])
                        P.add("scalar", exB, reads=["pb7", "shf"], writes=[f"erng{eB}"])
                        if 0 <= m <= 3:
                            def mkA(e, eA=eA, m=m):
                                return e.tensor_tensor(erng[eA][:], erng[eA][:],
                                                       masks[:, m, :], ALU.mult)
                            P.add("vector", mkA, reads=[f"erng{eA}", "masks"],
                                  writes=[f"erng{eA}"])
                            def mkB(e, eB=eB, m=m):
                                return e.tensor_tensor(erng[eB][:], erng[eB][:],
                                                       masks[:, m, :], ALU.mult)
                            P.add("vector", mkB, reads=[f"erng{eB}", "masks"],
                                  writes=[f"erng{eB}"])
                        bA = hA - base
                        bB = hB - base
                        def pvA(e, hA=hA, sl=sl, eA=eA, bA=bA, j=j):
                            return e.matmul(pb[bA][0:65, :],
                                            vrng[sl][:, 65 * hA:65 * hA + 65],
                                            erng[eA][:], start=(j == 0), stop=(j == jlast))
                        P.add("tensor", pvA, reads=[f"vrng{sl}", f"erng{eA}"],
                              writes=[pname[bA]])
                        def pvB(e, hB=hB, sl=sl, eB=eB, bB=bB, j=j):
                            return e.matmul(pb[bB][0:65, :],
                                            vrng[sl][:, 65 * hB:65 * hB + 65],
                                            erng[eB][:], start=(j == 0), stop=(j == jlast))
                        P.add("tensor", pvB, reads=[f"vrng{sl}", f"erng{eB}"],
                              writes=[pname[bB]])
                for h in heads:
                    bk = h - base
                    def rc(e, bk=bk):
                        return e.reciprocal(row(R_RS), pb[bk][64:65, :])
                    P.add("vector", rc, reads=[pname[bk]], writes=["rows"])
                    P.add("tensor", lambda e: e.matmul(pb6[:], ones1[:], row(R_RS),
                                                       start=True, stop=True),
                          reads=["rows", "ones1"], writes=["pb6"])
                    P.add("vector", lambda e: e.tensor_copy(rsbc[:], pb6[:]),
                          reads=["pb6"], writes=["rsbc"])
                    po = 64 * (h % 2)
                    dc = h // 2
                    def fin1(e, bk=bk, po=po):
                        return e.tensor_tensor(tmp1[po:po + 64, :], pb[bk][0:64, :],
                                               rsbc[po:po + 64, :], ALU.mult)
                    P.add("vector", fin1, reads=[pname[bk], "rsbc"], writes=["tmp1"])
                    def fin2(e, dc=dc, po=po):
                        return e.tensor_tensor(acc[po:po + 64, dc, :],
                                               acc[po:po + 64, dc, :],
                                               tmp1[po:po + 64, :], ALU.add)
                    P.add("vector", fin2, reads=["tmp1", "acc"], writes=["acc"])

        def dump(i, name, tile, res):
            if DEBUG_DUMPS and i == 0:
                P.add("gpsimd", lambda e: e.dma_start(out=dbg[name], in_=tile[:]),
                      dma=True, reads=[res], writes=[name])

        def build_block_a(i):
            s = i % 2
            if i == 0:
                def ldx(e):
                    return e.dma_start(out=xt[:, 0, :, :], in_=xin[:, :, 0, :].transpose(
                        [1, 0, 2]))
                P.add("sync", ldx, dma=True, writes=["xt0"])
            if i + 1 < NB:
                def ldx2(e, i=i):
                    return e.dma_start(out=xt[:, (i + 1) % 2, :, :],
                                       in_=xin[:, :, i + 1, :].transpose([1, 0, 2]))
                P.add("sync", ldx2, dma=True, writes=[f"xt{(i + 1) % 2}"])

            xres = f"xt{s}"
            xv = lambda dc, s=s: xt[:, s, dc, :]
            layernorm(xv, xres)
            dump(i, "dbg_normed1", normed, "normed")
            router()
            dump(i, "dbg_gbc", gbc, "gbc")
            for dc in range(DC):
                def cp(e, dc=dc, s=s):
                    return e.tensor_copy(acc[:, dc, :], xt[:, s, dc, :])
                P.add("vector", cp, reads=[xres], writes=["acc"])
            ffn(i)
            dump(i, "dbg_h", sb["hbuf"], "hbuf")
            dump(i, "dbg_x2", acc, "acc")
            av = lambda dc: acc[:, dc, :]
            layernorm(av, "acc")
            def stx2(e, i=i):
                return e.dma_start(out=x2d[:, :, i, :].transpose([1, 0, 2]), in_=acc[:])
            P.add("gpsimd", stx2, dma=True, reads=["acc"], writes=["x2dd"])
            qkv(i)
            def stq(e, i=i):
                return e.dma_start(out=qdr[:, :, i, :].transpose([1, 0, 2]), in_=qbf[:])
            P.add("gpsimd", stq, dma=True, reads=["qbf"], writes=["qdrd"])
            dump(i, "dbg_q", qbf, "qbf")
            dump(i, "dbg_k", kst, "kst")
            dump(i, "dbg_v", vst, "vst")

        def build_block_b(i):
            def ldx2b(e, i=i):
                return e.dma_start(out=acc[:], in_=x2d[:, :, i, :].transpose([1, 0, 2]))
            P.add("sync", ldx2b, dma=True, reads=["x2dd"], writes=["acc"])
            def ldq(e, i=i):
                return e.dma_start(out=qbf[:], in_=qdr[:, :, i, :].transpose([1, 0, 2]))
            P.add("sync", ldq, dma=True, reads=["qdrd"], writes=["qbf"])
            attention(i)
            dump(i, "dbg_x3", acc, "acc")
            av = lambda dc: acc[:, dc, :]
            layernorm(av, "acc")
            router()
            ffn(i)
            def sto(e, i=i):
                dst = outap[:, :, i, :].transpose([1, 0, 2])
                return e.dma_start(out=dst, in_=acc[:])
            P.add("gpsimd", sto, dma=True, reads=["acc"], writes=["outd"])

        prologue()
        for i in range(NB):
            build_block_a(i)
        kscr_pass()
        if DEBUG_DUMPS:
            P.add("gpsimd", lambda e: e.dma_start(out=dbg["dbg_kdr"], in_=kdr),
                  dma=True, reads=["kdrd"], writes=["dbg_kdr"])
            def _dkscr(e):
                return e.dma_start(out=dbg["dbg_kscr"], in_=kscr[0:2, 0:4])
            P.add("gpsimd", _dkscr, dma=True, reads=["kscrd"], writes=["dbg_kscr"])
        for i in range(NB):
            build_block_b(i)

        P.resolve()

        import contextlib
        with contextlib.ExitStack() as es:
            sems = {}
            for ki, key in enumerate(P.sem_keys):
                sems[key] = es.enter_context(nc.semaphore(f"sm{ki}"))
            block = es.enter_context(nc.Block())

            @block.sync
            def _(e):
                P.emit("sync", e, sems)

            @block.scalar
            def _(e):
                P.emit("scalar", e, sems)

            @block.vector
            def _(e):
                P.emit("vector", e, sems)

            @block.tensor
            def _(e):
                P.emit("tensor", e, sems)

            @block.gpsimd
            def _(e):
                P.emit("gpsimd", e, sems)
                e.wait_ge(sems[("r", "outd")], P.dma_final["outd"])

    return nc, sbuf_left, len(P.ops)


_NC_CACHE = None


def kernel(L0_attn_x, L0_attn_qkv_weights, L0_ffn_router_W, L0_ffn_router_b,
           L0_ffn_W1, L0_ffn_b1, L0_ffn_W2, L0_ffn_b2):
    global _NC_CACHE
    x = np.asarray(L0_attn_x, dtype=np.float32)
    qkvw = np.asarray(L0_attn_qkv_weights, dtype=np.float32)
    rW = np.asarray(L0_ffn_router_W, dtype=np.float32)
    rb = np.asarray(L0_ffn_router_b, dtype=np.float32)
    W1 = np.asarray(L0_ffn_W1, dtype=np.float32)
    b1 = np.asarray(L0_ffn_b1, dtype=np.float32)
    W2 = np.asarray(L0_ffn_W2, dtype=np.float32)
    b2 = np.asarray(L0_ffn_b2, dtype=np.float32)

    if _NC_CACHE is None:
        _NC_CACHE = build_nc()
    nc, _, _ = _NC_CACHE

    qkvw_l = np.ascontiguousarray(qkvw.reshape(DC, 128, 3 * D))
    rwT_l = np.ascontiguousarray(rW.reshape(DC, 128, NE).transpose(1, 0, 2)).reshape(128, DC * NE)
    rb_l = np.ascontiguousarray(np.broadcast_to(rb[None, :], (128, NE)))
    b1T_l = np.ascontiguousarray(b1.reshape(NE, FCN, 128).transpose(2, 0, 1)).reshape(128, NE * FCN)
    b2T_l = np.ascontiguousarray(b2.reshape(NE, DC, 128).transpose(2, 0, 1)).reshape(128, NE * DC)
    w1_l = np.ascontiguousarray(W1.reshape(NE, DC, 128, DFF))
    w2_l = np.ascontiguousarray(W2.reshape(NE, FCN, 128, D))

    in_maps = []
    for b in range(8):
        x_fm = np.ascontiguousarray(x[b].T).reshape(DC, 128, NB, T)
        in_maps.append({
            "x_fm": x_fm, "qkvw": qkvw_l, "rwT": rwT_l, "rb_bc": rb_l,
            "b1T": b1T_l, "b2T": b2T_l, "w1": w1_l, "w2": w2_l,
        })
    res = run_bass_kernel_spmd(nc, in_maps, core_ids=list(range(8)))
    out = np.empty((8, S, D), np.float32)
    for b in range(8):
        o = res.results[b]["out_fm"]
        out[b] = o.transpose(2, 3, 0, 1).reshape(S, D)
    return out


# version 05
# speedup vs baseline: 1.0619x; 1.0212x over previous
import numpy as np
import ml_dtypes
import concourse.bass as bass
import concourse.mybir as mybir
from concourse.bass_utils import run_bass_kernel_spmd

F32 = mybir.dt.float32
BF16 = mybir.dt.bfloat16
AF = mybir.ActivationFunctionType
ALU = mybir.AluOpType
AX = mybir.AxisListType

D = 1024
S = 4096
NB = 8
T = 512
DC = 8
FCN = 32
NE = 8
NH = 16
DH = 64
NSUB = 4
DFF = 4096
EPS = 1e-5
EXP_SHIFT = -20.0

ENGINES = ("tensor", "vector", "scalar", "gpsimd", "sync")
DEBUG_DUMPS = False


class Prog:

    def __init__(self):
        self.ops = []
        self.last_writer = {}
        self.readers = {}

    def add(self, engine, fn, reads=(), writes=(), dma=False):
        idx = len(self.ops)
        deps = set()
        for r in reads:
            w = self.last_writer.get(r)
            if w is not None:
                deps.add(w)
        for w_ in writes:
            w = self.last_writer.get(w_)
            if w is not None:
                deps.add(w)
            for rd in self.readers.get(w_, ()):
                deps.add(rd)
        deps.discard(idx)
        wres = writes[0] if dma else None
        if dma:
            assert len(writes) == 1, "dma ops must write exactly one resource"
        self.ops.append(dict(engine=engine, fn=fn, dma=dma, deps=deps,
                             need_inc=False, idx=idx, wres=wres))
        for w_ in writes:
            self.last_writer[w_] = idx
            self.readers[w_] = []
        for r in reads:
            self.readers.setdefault(r, []).append(idx)
        return idx

    def resolve(self):
        for op in self.ops:
            kept = []
            for d in sorted(op["deps"]):
                p = self.ops[d]
                if p["engine"] == "tensor" and op["engine"] == "tensor" \
                        and not p["dma"] and not op["dma"]:
                    continue
                kept.append(d)
                if not p["dma"]:
                    p["need_inc"] = True
            op["edges"] = kept
        cc = {e: 0 for e in ENGINES}
        rc = {}
        for op in self.ops:
            e = op["engine"]
            if op["dma"]:
                r = op["wres"]
                rc[r] = rc.get(r, 0) + 1
                op["post"] = ("r", r, rc[r] * 16)
            elif op["need_inc"]:
                cc[e] += 1
                op["post"] = ("c", e, cc[e])
            else:
                op["post"] = None
        self.dma_final = {r: v * 16 for r, v in rc.items()}
        self.sem_keys = [("c", e) for e in ENGINES if e != "sync"] +                         [("r", r) for r in rc]
        know = {e: {} for e in ENGINES}
        for op in self.ops:
            k = know[op["engine"]]
            waits = {}
            for d in op["edges"]:
                p = self.ops[d]
                assert p["post"] is not None, (op["idx"], d)
                kind, pe, val = p["post"]
                sem = (kind, pe)
                if k.get(sem, 0) >= val:
                    continue
                if waits.get(sem, 0) < val:
                    waits[sem] = val
            for sem, val in waits.items():
                k[sem] = val
            op["waits"] = sorted(waits.items())
            if op["engine"] == "tensor" and op["post"] is not None and op["post"][0] == "c":
                k[("c", "tensor")] = op["post"][2]

    def emit(self, engine_name, eng, sems):
        for op in self.ops:
            if op["engine"] != engine_name:
                continue
            for (sem, val) in op["waits"]:
                eng.wait_ge(sems[sem], val)
            inst = op["fn"](eng)
            if op["dma"]:
                inst.then_inc(sems[("r", op["wres"])], 16)
            elif op["need_inc"]:
                if inst is None:
                    eng.nop().then_inc(sems[("c", engine_name)], 1)
                else:
                    inst.then_inc(sems[("c", engine_name)], 1)


def build_nc():
    nc = bass.Bass("TRN2", target_bir_lowering=False, debug=False)

    xin = nc.dram_tensor("x_fm", [DC, 128, NB, T], F32, kind="ExternalInput").ap()
    qkvw = nc.dram_tensor("qkvw", [DC, 128, 3 * D], F32, kind="ExternalInput").ap()
    rwT = nc.dram_tensor("rwT", [128, DC * NE], F32, kind="ExternalInput").ap()
    rbbc_in = nc.dram_tensor("rb_bc", [128, NE], F32, kind="ExternalInput").ap()
    b1T_in = nc.dram_tensor("b1T", [128, NE * FCN], F32, kind="ExternalInput").ap()
    b2T_in = nc.dram_tensor("b2T", [128, NE * DC], F32, kind="ExternalInput").ap()
    w1in = nc.dram_tensor("w1", [NE, DC, 128, DFF], F32, kind="ExternalInput").ap()
    w2in = nc.dram_tensor("w2", [NE, FCN, 128, D], F32, kind="ExternalInput").ap()
    outap = nc.dram_tensor("out_fm", [DC, 128, NB, T], F32, kind="ExternalOutput").ap()
    dbg = {}
    if DEBUG_DUMPS:
        for nm, shape, dt in [("dbg_normed1", [128, DC, T], BF16),
                              ("dbg_x2", [128, DC, T], F32),
                              ("dbg_gbc", [128, NE, T], BF16),
                              ("dbg_q", [128, DC, T], BF16),
                              ("dbg_k", [128, DC, T], BF16),
                              ("dbg_v", [128, NSUB, NH * 65], BF16),
                              ("dbg_x3", [128, DC, T], F32),
                              ("dbg_h", [128, FCN * T], BF16)]:
            dbg[nm] = nc.dram_tensor(nm, shape, dt, kind="ExternalOutput").ap()
        dbg["dbg_kdr"] = nc.dram_tensor("dbg_kdr", [DC, 128, S], BF16,
                                        kind="ExternalOutput").ap()
        dbg["dbg_kscr"] = nc.dram_tensor("dbg_kscr", [2, 4, 128, 128], BF16,
                                         kind="ExternalOutput").ap()

    w1b = [nc.dram_tensor(f"w1b{e}", [FCN, 128, DC, 128], BF16, kind="Internal").ap()
           for e in range(NE)]
    w2b = [nc.dram_tensor(f"w2b{e}", [FCN, 128, D], BF16, kind="Internal").ap()
           for e in range(NE)]
    qkvb = nc.dram_tensor("qkvb", [DC, 128, 3 * D], BF16, kind="Internal").ap()
    gsc = nc.dram_tensor("gsc", [1, NE, T], BF16, kind="Internal").ap()
    kdr = nc.dram_tensor("kdr", [DC, 128, S], BF16, kind="Internal").ap()
    vdr = nc.dram_tensor("vdr", [S, NH * 65], BF16, kind="Internal").ap()
    x2d = nc.dram_tensor("x2d", [DC, 128, NB, T], F32, kind="Internal").ap()
    qdr = nc.dram_tensor("qdr", [DC, 128, NB, T], BF16, kind="Internal").ap()
    kscr = nc.dram_tensor("kscr", [NH // 2, FCN, 128, 128], BF16, kind="Internal").ap()

    sb = {}

    def sbuf(name, shape, dt):
        sb[name] = nc.alloc_sbuf_tensor(name, shape, dt)
        return sb[name]

    rwt = sbuf("rwt", [128, DC * NE], BF16)
    rbbc = sbuf("rbbc", [128, NE], F32)
    b1t = sbuf("b1t", [128, NE * FCN], F32)
    b2t = sbuf("b2t", [128, NE * DC], F32)
    iden = sbuf("iden", [128, 128], F32)
    iotap = sbuf("iotap", [128, 1], F32)
    iota8 = sbuf("iota8", [128, 8], F32)
    iom9 = sbuf("iom9", [128, 8], F32)
    onesc = sbuf("onesc", [128, 1], F32)
    ones1 = sbuf("ones1", [1, 128], F32)
    masks = sbuf("masks", [128, 4, T], BF16)
    iotaq = sbuf("iotaq", [128, T], F32)

    xt = sbuf("xt", [128, 2, DC, T], F32)
    acc = sbuf("acc", [128, DC, T], F32)
    normed = sbuf("normed", [128, DC, T], BF16)
    hbuf = sbuf("hbuf", [128, FCN * T], BF16)
    sqv = sb["hbuf"].bitcast(F32)
    gbc = sbuf("gbc", [128, NE, T], BF16)
    qbf = sbuf("qbf", [128, DC, T], BF16)
    kst = sbuf("kst", [128, DC, T], BF16)
    vst = sbuf("vst", [128, NSUB, NH * 65], BF16)

    w1r = [sbuf(f"w1r{s}", [128, DC, 128], BF16) for s in range(4)]
    w2r = [sbuf(f"w2r{s}", [128, DC, 128], BF16) for s in range(4)]
    qkr = [sbuf(f"qkr{s}", [128, 1024], BF16) for s in range(2)]
    ksrng = [sbuf(f"ksrng{s}", [128, 128], BF16) for s in range(6)]
    katile = [sbuf(f"katile{s}", [128, 128], BF16) for s in range(2)]
    kstg = [sbuf(f"kstg{s}", [128, 128], BF16) for s in range(2)]
    iden_bf = sbuf("iden_bf", [128, 128], BF16)
    vrng = [sbuf(f"vrng{s}", [128, NH * 65], BF16) for s in range(4)]
    erng = [sbuf(f"erng{s}", [128, T], BF16) for s in range(4)]
    cstg = [sbuf(f"cstg{s}", [128, 2048], BF16) for s in range(2)]

    rows = sbuf("rows", [1, 7 * T], F32)
    grow8 = sbuf("grow8", [1, NE * T], BF16)
    eps1 = sbuf("eps1", [1, 1], F32)
    shf = sbuf("shf", [128, 1], F32)
    ones1b = sbuf("ones1b", [1, 128], BF16)
    a_sb = sbuf("a_sb", [128, T], F32)
    b_sb = sbuf("b_sb", [128, T], F32)
    rsbc = sbuf("rsbc", [128, T], F32)
    tmp1 = sbuf("tmp1", [128, T], F32)
    tmp2 = sbuf("tmp2", [128, T], F32)
    lgt = sbuf("lgt", [128, 8], F32)
    lmax = sbuf("lmax", [128, 1], F32)
    negm = sbuf("negm", [128, 1], F32)
    expv = sbuf("expv", [128, 8], F32)
    sume = sbuf("sume", [128, 1], F32)
    gval = sbuf("gval", [128, 1], F32)
    meq = sbuf("meq", [128, 8], F32)
    selv = sbuf("selv", [128, 8], F32)
    emin = sbuf("emin", [128, 1], F32)
    g8 = sbuf("g8", [128, 8], F32)
    grs = sbuf("grs", [8, T], F32)

    sbuf_left = nc.sbuf_bytes_remaining
    assert sbuf_left >= 0, sbuf_left

    R_MU, R_MSQ, R_VAR, R_LNV, R_A, R_B, R_RS = range(7)

    def row(i):
        return sb["rows"][0:1, i * T:(i + 1) * T]

    P = Prog()

    with (
        nc.psum_tensor("pb0", [128, T], F32) as pb0,
        nc.psum_tensor("pb1", [128, T], F32) as pb1,
        nc.psum_tensor("pb2", [128, T], F32) as pb2,
        nc.psum_tensor("pb3", [128, T], F32) as pb3,
        nc.psum_tensor("pb4", [128, T], F32) as pb4,
        nc.psum_tensor("pb5", [128, T], F32) as pb5,
        nc.psum_tensor("pb6", [128, T], F32) as pb6,
        nc.psum_tensor("pb7", [128, T], F32) as pb7,
    ):
        pb = [pb0, pb1, pb2, pb3, pb4, pb5, pb6, pb7]
        pname = [f"pb{i}" for i in range(8)]

        def prologue():
            P.add("gpsimd", lambda e: e.iota(iotap[:], [[0, 1]], channel_multiplier=1,
                                             allow_small_or_imprecise_dtypes=True),
                  writes=["iotap"])
            P.add("gpsimd", lambda e: e.iota(iota8[:], [[1, 8]], channel_multiplier=0,
                                             allow_small_or_imprecise_dtypes=True),
                  writes=["iota8"])
            P.add("gpsimd", lambda e: e.iota(iotaq[:], [[1, T]], channel_multiplier=0,
                                             allow_small_or_imprecise_dtypes=True),
                  writes=["iotaq"])
            P.add("vector", lambda e: e.tensor_scalar(iom9[:], iota8[:], -9.0, None,
                                                      ALU.add),
                  reads=["iota8"], writes=["iom9"])
            P.add("vector", lambda e: e.memset(onesc[:], 1.0), writes=["onesc"])
            P.add("vector", lambda e: e.memset(ones1[:], 1.0), writes=["ones1"])
            P.add("vector", lambda e: e.memset(ones1b[:], 1.0), writes=["ones1b"])
            P.add("vector", lambda e: e.memset(eps1[:], EPS), writes=["eps1"])
            P.add("vector", lambda e: e.memset(shf[:], EXP_SHIFT), writes=["shf"])
            P.add("vector", lambda e: e.memset(vst[:], 1.0), writes=["vst"])
            P.add("gpsimd", lambda e: e.iota(iden[:], [[1, 128]], channel_multiplier=0,
                                             allow_small_or_imprecise_dtypes=True),
                  writes=["iden"])
            P.add("vector", lambda e: e.tensor_scalar(iden[:], iden[:], iotap[:], None,
                                                      ALU.is_equal),
                  reads=["iotap"], writes=["iden"])
            P.add("vector", lambda e: e.tensor_copy(iden_bf[:], iden[:]),
                  reads=["iden"], writes=["iden_bf"])
            for m in range(4):
                def mk(e, m=m):
                    return e.tensor_scalar(negm[:], iotap[:], float(128 * m), None, ALU.add)
                P.add("vector", mk, reads=["iotap"], writes=["negm"])
                def mk2(e, m=m):
                    return e.tensor_scalar(masks[:, m, :], iotaq[:], negm[:], None, ALU.is_ge)
                P.add("vector", mk2, reads=["iotaq", "negm"], writes=["masks"])
            P.add("gpsimd", lambda e: e.dma_start(out=rwt[:], in_=rwT), dma=True,
                  writes=["rwt"])
            P.add("sync", lambda e: e.dma_start(out=rbbc[:], in_=rbbc_in), dma=True,
                  writes=["rbbc"])
            P.add("sync", lambda e: e.dma_start(out=b1t[:], in_=b1T_in), dma=True,
                  writes=["b1t"])
            P.add("sync", lambda e: e.dma_start(out=b2t[:], in_=b2T_in), dma=True,
                  writes=["b2t"])
            cast_expert(0, gate=None)
            cast_expert(1, gate=None)

        slc = [0]

        def cast_expert(e_, gate):
            if True:
                for fs2 in range(FCN // 2):
                    sl = slc[0] % 2
                    slc[0] += 1
                    def ld(eng, e_=e_, fs2=fs2, sl=sl):
                        src = w1in[e_, :, :, fs2 * 256:(fs2 + 1) * 256].transpose(
                            [1, 0, 2])
                        return eng.dma_start(out=cstg[sl][:].rearrange(
                            "p (dc f) -> p dc f", dc=DC), in_=src)
                    rds = [gate] if (fs2 == 0 and gate) else []
                    P.add("gpsimd", ld, dma=True, reads=rds, writes=[f"cstg{sl}"])
                    def st(eng, e_=e_, fs2=fs2, sl=sl):
                        dst = w1b[e_][2 * fs2:2 * fs2 + 2].transpose([1, 2, 0, 3])
                        return eng.dma_start(out=dst, in_=cstg[sl][:].rearrange(
                            "p (dc fcp f) -> p dc fcp f", dc=DC, fcp=2))
                    P.add("sync", st, dma=True, reads=[f"cstg{sl}"], writes=[f"w1bd{e_}"])
                for fs2 in range(FCN // 2):
                    sl = slc[0] % 2
                    slc[0] += 1
                    def ld2(eng, e_=e_, fs2=fs2, sl=sl):
                        src = w2in[e_, 2 * fs2:2 * fs2 + 2, :, :].transpose([1, 0, 2])
                        return eng.dma_start(out=cstg[sl][:].rearrange(
                            "p (fcp d) -> p fcp d", fcp=2), in_=src)
                    P.add("gpsimd", ld2, dma=True, writes=[f"cstg{sl}"])
                    def st2(eng, e_=e_, fs2=fs2, sl=sl):
                        dst = w2b[e_][2 * fs2:2 * fs2 + 2].transpose([1, 0, 2])
                        return eng.dma_start(out=dst, in_=cstg[sl][:].rearrange(
                            "p (fcp d) -> p fcp d", fcp=2))
                    P.add("sync", st2, dma=True, reads=[f"cstg{sl}"], writes=[f"w2bd{e_}"])
        def cast_qkv():
            for dc in range(DC):
                for qs in range(3):
                    sl = slc[0] % 2
                    slc[0] += 1
                    def ld3(eng, dc=dc, qs=qs, sl=sl):
                        return eng.dma_start(out=cstg[sl][:, 0:1024],
                                             in_=qkvw[dc][:, qs * 1024:(qs + 1) * 1024])
                    P.add("gpsimd", ld3, dma=True, writes=[f"cstg{sl}"])
                    def st3(eng, dc=dc, qs=qs, sl=sl):
                        return eng.dma_start(out=qkvb[dc][:, qs * 1024:(qs + 1) * 1024],
                                             in_=cstg[sl][:, 0:1024])
                    P.add("sync", st3, dma=True, reads=[f"cstg{sl}"], writes=["qkvbd"])

        def layernorm(src_view, src_res):
            for dc in range(DC):
                def sq(e, dc=dc):
                    return e.tensor_tensor(sqv[:, dc * T:(dc + 1) * T], src_view(dc),
                                           src_view(dc), ALU.mult)
                P.add("vector", sq, reads=[src_res], writes=["hbuf"])
            for dc in range(DC):
                def mm(e, dc=dc):
                    return e.matmul(pb6[0:1, :], onesc[:], src_view(dc),
                                    start=(dc == 0), stop=(dc == DC - 1))
                P.add("tensor", mm, reads=[src_res, "onesc"], writes=["pb6"])
            for dc in range(DC):
                def mm2(e, dc=dc):
                    return e.matmul(pb7[0:1, :], onesc[:], sqv[:, dc * T:(dc + 1) * T],
                                    start=(dc == 0), stop=(dc == DC - 1))
                P.add("tensor", mm2, reads=["hbuf", "onesc"], writes=["pb7"])
            P.add("vector", lambda e: e.tensor_scalar(row(R_MU), pb6[0:1, :], 1.0 / D, None,
                                                      ALU.mult),
                  reads=["pb6"], writes=["rows"])
            P.add("vector", lambda e: e.tensor_scalar(row(R_MSQ), pb7[0:1, :], 1.0 / D, None,
                                                      ALU.mult),
                  reads=["pb7"], writes=["rows"])
            P.add("vector", lambda e: e.scalar_tensor_tensor(row(R_VAR), row(R_MU), -1.0,
                                                             row(R_MU), ALU.mult, ALU.mult),
                  reads=["rows"], writes=["rows"])
            P.add("vector", lambda e: e.tensor_tensor(row(R_VAR), row(R_VAR), row(R_MSQ),
                                                      ALU.add),
                  reads=["rows"], writes=["rows"])
            P.add("scalar", lambda e: e.activation(row(R_LNV), row(R_VAR), AF.Ln,
                                                   bias=eps1[:]),
                  reads=["rows", "eps1"], writes=["rows"])
            P.add("scalar", lambda e: e.activation(row(R_A), row(R_LNV), AF.Exp, scale=-0.5),
                  reads=["rows"], writes=["rows"])
            P.add("vector", lambda e: e.scalar_tensor_tensor(row(R_B), row(R_MU), -1.0,
                                                             row(R_A), ALU.mult, ALU.mult),
                  reads=["rows"], writes=["rows"])
            P.add("tensor", lambda e: e.matmul(pb6[:], ones1[:], row(R_A), start=True,
                                               stop=True),
                  reads=["rows", "ones1"], writes=["pb6"])
            P.add("tensor", lambda e: e.matmul(pb7[:], ones1[:], row(R_B), start=True,
                                               stop=True),
                  reads=["rows", "ones1"], writes=["pb7"])
            P.add("vector", lambda e: e.tensor_copy(a_sb[:], pb6[:]),
                  reads=["pb6"], writes=["a_sb"])
            P.add("vector", lambda e: e.tensor_copy(b_sb[:], pb7[:]),
                  reads=["pb7"], writes=["b_sb"])
            for dc in range(DC):
                def n1(e, dc=dc):
                    return e.tensor_tensor(tmp1[:], src_view(dc), a_sb[:], ALU.mult)
                P.add("vector", n1, reads=[src_res, "a_sb"], writes=["tmp1"])
                def n2(e, dc=dc):
                    return e.tensor_tensor(normed[:, dc, :], tmp1[:], b_sb[:], ALU.add)
                P.add("vector", n2, reads=["tmp1", "b_sb"], writes=["normed"])

        def router():
            tbanks = [2, 3, 4, 5]
            for sub in range(NSUB):
                for dc in range(DC):
                    def mm(e, sub=sub, dc=dc):
                        return e.matmul(pb6[:, 0:8],
                                        normed[:, dc, sub * 128:(sub + 1) * 128],
                                        rwt[:, dc * 8:(dc + 1) * 8],
                                        start=(dc == 0), stop=(dc == DC - 1))
                    P.add("tensor", mm, reads=["normed", "rwt"], writes=["pb6"])
                P.add("vector", lambda e: e.tensor_tensor(lgt[:], pb6[:, 0:8], rbbc[:],
                                                          ALU.add),
                      reads=["pb6", "rbbc"], writes=["lgt"])
                P.add("vector", lambda e: e.tensor_reduce(lmax[:], lgt[:], AX.X, ALU.max),
                      reads=["lgt"], writes=["lmax"])
                P.add("vector", lambda e: e.tensor_scalar(negm[:], lmax[:], -1.0, None,
                                                          ALU.mult),
                      reads=["lmax"], writes=["negm"])
                P.add("scalar", lambda e: e.activation(expv[:], lgt[:], AF.Exp,
                                                       bias=negm[:]),
                      reads=["lgt", "negm"], writes=["expv"])
                P.add("vector", lambda e: e.tensor_reduce(sume[:], expv[:], AX.X, ALU.add),
                      reads=["expv"], writes=["sume"])
                P.add("vector", lambda e: e.reciprocal(gval[:], sume[:]),
                      reads=["sume"], writes=["gval"])
                P.add("vector", lambda e: e.tensor_scalar(gval[:], gval[:], 0.5, None,
                                                          ALU.mult),
                      reads=["gval"], writes=["gval"])
                P.add("vector", lambda e: e.tensor_scalar(meq[:], lgt[:], lmax[:], None,
                                                          ALU.is_equal),
                      reads=["lgt", "lmax"], writes=["meq"])
                P.add("vector", lambda e: e.tensor_tensor(selv[:], meq[:], iom9[:],
                                                          ALU.mult),
                      reads=["meq", "iom9"], writes=["selv"])
                P.add("vector", lambda e: e.tensor_scalar(selv[:], selv[:], 9.0, None,
                                                          ALU.add),
                      reads=["selv"], writes=["selv"])
                P.add("vector", lambda e: e.tensor_reduce(emin[:], selv[:], AX.X, ALU.min),
                      reads=["selv"], writes=["emin"])
                P.add("vector", lambda e: e.tensor_scalar(meq[:], iota8[:], emin[:], None,
                                                          ALU.is_equal),
                      reads=["iota8", "emin"], writes=["meq"])
                P.add("vector", lambda e: e.tensor_scalar(g8[:], meq[:], gval[:], None,
                                                          ALU.mult),
                      reads=["meq", "gval"], writes=["g8"])
                bk = tbanks[sub]
                def tr(e, sub=sub, bk=bk):
                    return e.transpose(pb[bk][0:8, 0:128], g8[:], iden[:])
                P.add("tensor", tr, reads=["g8", "iden"], writes=[pname[bk]])
                def ev(e, sub=sub, bk=bk):
                    return e.tensor_copy(grs[:, sub * 128:(sub + 1) * 128],
                                         pb[bk][0:8, 0:128])
                P.add("vector", ev, reads=[pname[bk]], writes=["grs"])
            P.add("gpsimd", lambda e: e.dma_start(out=gsc[0], in_=grs[:]), dma=True,
                  reads=["grs"], writes=["gscd"])
            def mvrow(e):
                dst = grow8[0:1, :].rearrange("o (e t) -> o e t", e=8)
                return e.dma_start(out=dst, in_=gsc)
            P.add("sync", mvrow, dma=True, reads=["gscd"], writes=["grow8"])
            for e_ in range(NE):
                bk = 6 if e_ % 2 == 0 else 7
                def mm(e, e_=e_, bk=bk):
                    return e.matmul(pb[bk][:], ones1b[:],
                                    grow8[0:1, e_ * T:(e_ + 1) * T], start=True, stop=True)
                P.add("tensor", mm, reads=["grow8", "ones1b"], writes=[pname[bk]])
                def ev2(e, e_=e_, bk=bk):
                    return e.tensor_copy(gbc[:, e_, :], pb[bk][:])
                P.add("vector", ev2, reads=[pname[bk]], writes=["gbc"])

        def ffn(iblk, on_expert_done=None):
            def ld_w1(e_, fc):
                sl = fc % 4
                def ld(eng, e_=e_, fc=fc, sl=sl):
                    return eng.dma_start(out=w1r[sl][:], in_=w1b[e_][fc])
                P.add("sync", ld, dma=True, reads=[f"w1bd{e_}"], writes=[f"w1r{sl}"])

            def ld_w2(e_, fc):
                sl = fc % 4
                def ld2(eng, e_=e_, fc=fc, sl=sl):
                    return eng.dma_start(out=w2r[sl][:].rearrange(
                        "p dc f -> p (dc f)"), in_=w2b[e_][fc])
                P.add("sync", ld2, dma=True, reads=[f"w2bd{e_}"], writes=[f"w2r{sl}"])

            for e_ in range(NE):
                for fc in range(4):
                    ld_w1(e_, fc)
                for fc in range(FCN):
                    sl = fc % 4
                    hb = 6 if fc % 2 == 0 else 7
                    for dc in range(DC):
                        def mm(e, fc=fc, dc=dc, sl=sl, hb=hb):
                            return e.matmul(pb[hb][:], w1r[sl][:, dc, :], normed[:, dc, :],
                                            start=(dc == 0), stop=(dc == DC - 1))
                        P.add("tensor", mm, reads=[f"w1r{sl}", "normed"], writes=[pname[hb]])
                    if fc + 4 < FCN:
                        ld_w1(e_, fc + 4)
                    def relu(e, e_=e_, fc=fc, hb=hb):
                        col = e_ * FCN + fc
                        return e.activation(hbuf[:, fc * T:(fc + 1) * T], pb[hb][:],
                                            AF.Relu, bias=b1t[:, col:col + 1])
                    P.add("scalar", relu, reads=[pname[hb], "b1t"], writes=["hbuf"])
                for fc in range(4):
                    ld_w2(e_, fc)
                for half in range(2):
                    for fc in range(FCN):
                        sl = fc % 4
                        for dci in range(4):
                            dc = half * 4 + dci
                            yb = 2 + dci
                            def mm2(e, fc=fc, dc=dc, sl=sl, yb=yb):
                                return e.matmul(pb[yb][:], w2r[sl][:, dc, :],
                                                hbuf[:, fc * T:(fc + 1) * T],
                                                start=(fc == 0), stop=(fc == FCN - 1))
                            P.add("tensor", mm2, reads=[f"w2r{sl}", "hbuf"],
                                  writes=[pname[yb]])
                        if fc + 4 < FCN:
                            ld_w2(e_, fc + 4)
                    if half == 0:
                        for fc in range(4):
                            ld_w2(e_, fc)
                    for dci in range(4):
                        dc = half * 4 + dci
                        yb = 2 + dci
                        def ev(e, e_=e_, dc=dc, yb=yb):
                            col = e_ * DC + dc
                            return e.scalar_tensor_tensor(tmp2[:], pb[yb][:],
                                                          b2t[:, col:col + 1],
                                                          gbc[:, e_, :], ALU.add, ALU.mult)
                        P.add("vector", ev, reads=[pname[yb], "b2t", "gbc"], writes=["tmp2"])
                        def ac(e, dc=dc):
                            return e.tensor_tensor(acc[:, dc, :], acc[:, dc, :], tmp2[:],
                                                   ALU.add)
                        P.add("vector", ac, reads=["tmp2", "acc"], writes=["acc"])
                if on_expert_done is not None:
                    on_expert_done(e_)

        def qkv(iblk):
            ocgroups = [list(range(0, 6)), list(range(6, 12)), list(range(12, 16))]
            for grp in ocgroups:
                base = grp[0] * 128
                width = len(grp) * 128
                for dc in range(DC):
                    sl = dc % 2
                    def ld(eng, dc=dc, sl=sl, base=base, width=width):
                        return eng.dma_start(out=qkr[sl][:, 0:width],
                                             in_=qkvb[dc][:, base:base + width])
                    P.add("sync", ld, dma=True, reads=["qkvbd"], writes=[f"qkr{sl}"])
                    for k, oc in enumerate(grp):
                        def mm(e, dc=dc, sl=sl, k=k):
                            return e.matmul(pb[k][:], qkr[sl][:, k * 128:(k + 1) * 128],
                                            normed[:, dc, :], start=(dc == 0),
                                            stop=(dc == DC - 1))
                        P.add("tensor", mm, reads=[f"qkr{sl}", "normed"], writes=[pname[k]])
                for k, oc in enumerate(grp):
                    if oc < 8:
                        def ev(e, k=k, oc=oc):
                            return e.tensor_copy(qbf[:, oc, :], pb[k][:])
                        P.add("vector", ev, reads=[pname[k]], writes=["qbf"])
                    else:
                        def ev(e, k=k, oc=oc):
                            return e.tensor_copy(kst[:, oc - 8, :], pb[k][:])
                        P.add("vector", ev, reads=[pname[k]], writes=["kst"])
            for grp in ([(0, 0), (0, 1), (1, 0), (1, 1)], [(2, 0), (2, 1), (3, 0), (3, 1)]):
                for dc in range(DC):
                    sl = dc % 2
                    def ld2(eng, dc=dc, sl=sl):
                        return eng.dma_start(out=qkr[sl][:], in_=qkvb[dc][:, 2 * D:3 * D])
                    P.add("sync", ld2, dma=True, reads=["qkvbd"], writes=[f"qkr{sl}"])
                    for k, (sub, vh) in enumerate(grp):
                        def mm(e, dc=dc, sl=sl, sub=sub, vh=vh, k=k):
                            return e.matmul(pb[2 + k][:],
                                            normed[:, dc, sub * 128:(sub + 1) * 128],
                                            qkr[sl][:, vh * 512:(vh + 1) * 512],
                                            start=(dc == 0), stop=(dc == DC - 1))
                        P.add("tensor", mm, reads=[f"qkr{sl}", "normed"],
                              writes=[pname[2 + k]])
                for k, (sub, vh) in enumerate(grp):
                    def ev2(e, k=k, sub=sub, vh=vh):
                        dst = vst[:, sub, :].rearrange("p (h c) -> p h c", c=65)
                        dst = dst[:, 8 * vh:8 * vh + 8, 0:64]
                        src = pb[2 + k][:].rearrange("p (h c) -> p h c", c=64)
                        return e.tensor_copy(dst, src)
                    P.add("vector", ev2, reads=[pname[2 + k]], writes=["vst"])
            def stk(e, iblk=iblk):
                dst = kdr[:, :, iblk * T:(iblk + 1) * T].transpose([1, 0, 2])
                return e.dma_start(out=dst, in_=kst[:].rearrange("p dc t -> p dc t"))
            P.add("gpsimd", stk, dma=True, reads=["kst"], writes=["kdrd"])
            def stv(e, iblk=iblk):
                dst = vdr[iblk * T:(iblk + 1) * T, :].rearrange("(sub p) c -> p sub c", p=128)
                return e.dma_start(out=dst, in_=vst[:])
            P.add("gpsimd", stv, dma=True, reads=["vst"], writes=["vdrd"])

        def kscr_pass():
            cnt = [0]
            for hp in range(NH // 2):
                for jc in range(FCN):
                    sl = cnt[0] % 2
                    hb = 6 if cnt[0] % 2 == 0 else 7
                    cnt[0] += 1
                    def ld(e, hp=hp, jc=jc, sl=sl):
                        return e.dma_start(
                            out=katile[sl][:],
                            in_=kdr[hp, :, :].rearrange(
                                "(hh pr) (m c) -> hh pr m c", hh=2, c=64)[
                                :, 2 * jc:2 * jc + 2, :, :].transpose([1, 2, 0, 3]))
                    P.add("scalar", ld, dma=True, reads=["kdrd"], writes=[f"katile{sl}"])
                    def tr(e, sl=sl, hb=hb):
                        out_bf = pb[hb].bitcast(BF16)[0:128, 0:128]
                        return e.transpose(out_bf, katile[sl][:], iden_bf[:])
                    P.add("tensor", tr, reads=[f"katile{sl}", "iden_bf"], writes=[pname[hb]])
                    def ev(e, sl=sl, hb=hb):
                        return e.tensor_copy(kstg[sl][:], pb[hb].bitcast(BF16)[0:128, 0:128])
                    P.add("vector", ev, reads=[pname[hb]], writes=[f"kstg{sl}"])
                    def st(e, hp=hp, jc=jc, sl=sl):
                        return e.dma_start(out=kscr[hp, jc], in_=kstg[sl][:])
                    P.add("gpsimd", st, dma=True, reads=[f"kstg{sl}"], writes=["kscrd"])

        def attention(iblk):
            jlast = 4 * iblk + 3
            passes = [list(range(0, 6)), list(range(6, 12)), list(range(12, 16))]
            for heads in passes:
                npair = len(heads) // 2
                base = heads[0]
                kcnt = [0]
                kslot = {}
                def ldkv(j, pairs_hp):
                    sl = j % 4
                    def f2(e, j=j, sl=sl):
                        return e.dma_start(out=vrng[sl][:], in_=vdr[j * 128:(j + 1) * 128, :])
                    P.add("scalar", f2, dma=True, reads=["vdrd"], writes=[f"vrng{sl}"])
                    for hp in pairs_hp:
                        ks = kcnt[0] % 6
                        kcnt[0] += 1
                        kslot[(j, hp)] = ks
                        def f3(e, j=j, hp=hp, ks=ks):
                            return e.dma_start(out=ksrng[ks][:], in_=kscr[hp, j])
                        P.add("scalar", f3, dma=True, reads=["kscrd"],
                              writes=[f"ksrng{ks}"])
                pairs_hp = [(base // 2) + pi for pi in range(npair)]
                ldkv(0, pairs_hp)
                esl = 0
                for j in range(jlast + 1):
                    if j + 1 <= jlast:
                        ldkv(j + 1, pairs_hp)
                    sl = j % 4
                    m = j - 4 * iblk
                    for pi in range(npair):
                        c = (base // 2) + pi
                        hA, hB = 2 * c, 2 * c + 1
                        eA, eB = esl % 4, (esl + 1) % 4
                        esl += 2
                        ks = kslot[(j, c)]
                        def mmA(e, c=c, ks=ks):
                            return e.matmul(pb6[:], ksrng[ks][0:64, :],
                                            qbf[0:64, c, :], start=True, stop=True)
                        P.add("tensor", mmA, reads=[f"ksrng{ks}", "qbf"], writes=["pb6"])
                        def mmB(e, c=c, ks=ks):
                            return e.matmul(pb7[:], ksrng[ks][64:128, :],
                                            qbf[64:128, c, :], start=True, stop=True)
                        P.add("tensor", mmB, reads=[f"ksrng{ks}", "qbf"], writes=["pb7"])
                        def exA(e, eA=eA):
                            return e.activation(erng[eA][:], pb6[:], AF.Exp, scale=0.125,
                                                bias=shf[:])
                        P.add("scalar", exA, reads=["pb6", "shf"], writes=[f"erng{eA}"])
                        def exB(e, eB=eB):
                            return e.activation(erng[eB][:], pb7[:], AF.Exp, scale=0.125,
                                                bias=shf[:])
                        P.add("scalar", exB, reads=["pb7", "shf"], writes=[f"erng{eB}"])
                        if 0 <= m <= 3:
                            def mkA(e, eA=eA, m=m):
                                return e.tensor_tensor(erng[eA][:], erng[eA][:],
                                                       masks[:, m, :], ALU.mult)
                            P.add("vector", mkA, reads=[f"erng{eA}", "masks"],
                                  writes=[f"erng{eA}"])
                            def mkB(e, eB=eB, m=m):
                                return e.tensor_tensor(erng[eB][:], erng[eB][:],
                                                       masks[:, m, :], ALU.mult)
                            P.add("vector", mkB, reads=[f"erng{eB}", "masks"],
                                  writes=[f"erng{eB}"])
                        bA = hA - base
                        bB = hB - base
                        def pvA(e, hA=hA, sl=sl, eA=eA, bA=bA, j=j):
                            return e.matmul(pb[bA][0:65, :],
                                            vrng[sl][:, 65 * hA:65 * hA + 65],
                                            erng[eA][:], start=(j == 0), stop=(j == jlast))
                        P.add("tensor", pvA, reads=[f"vrng{sl}", f"erng{eA}"],
                              writes=[pname[bA]])
                        def pvB(e, hB=hB, sl=sl, eB=eB, bB=bB, j=j):
                            return e.matmul(pb[bB][0:65, :],
                                            vrng[sl][:, 65 * hB:65 * hB + 65],
                                            erng[eB][:], start=(j == 0), stop=(j == jlast))
                        P.add("tensor", pvB, reads=[f"vrng{sl}", f"erng{eB}"],
                              writes=[pname[bB]])
                for h in heads:
                    bk = h - base
                    def rc(e, bk=bk):
                        return e.reciprocal(row(R_RS), pb[bk][64:65, :])
                    P.add("vector", rc, reads=[pname[bk]], writes=["rows"])
                    P.add("tensor", lambda e: e.matmul(pb6[:], ones1[:], row(R_RS),
                                                       start=True, stop=True),
                          reads=["rows", "ones1"], writes=["pb6"])
                    P.add("vector", lambda e: e.tensor_copy(rsbc[:], pb6[:]),
                          reads=["pb6"], writes=["rsbc"])
                    po = 64 * (h % 2)
                    dc = h // 2
                    def fin1(e, bk=bk, po=po):
                        return e.tensor_tensor(tmp1[po:po + 64, :], pb[bk][0:64, :],
                                               rsbc[po:po + 64, :], ALU.mult)
                    P.add("vector", fin1, reads=[pname[bk], "rsbc"], writes=["tmp1"])
                    def fin2(e, dc=dc, po=po):
                        return e.tensor_tensor(acc[po:po + 64, dc, :],
                                               acc[po:po + 64, dc, :],
                                               tmp1[po:po + 64, :], ALU.add)
                    P.add("vector", fin2, reads=["tmp1", "acc"], writes=["acc"])

        def dump(i, name, tile, res):
            if DEBUG_DUMPS and i == 0:
                P.add("gpsimd", lambda e: e.dma_start(out=dbg[name], in_=tile[:]),
                      dma=True, reads=[res], writes=[name])

        def build_block_a(i):
            s = i % 2
            if i == 0:
                def ldx(e):
                    return e.dma_start(out=xt[:, 0, :, :], in_=xin[:, :, 0, :].transpose(
                        [1, 0, 2]))
                P.add("sync", ldx, dma=True, writes=["xt0"])
            if i + 1 < NB:
                def ldx2(e, i=i):
                    return e.dma_start(out=xt[:, (i + 1) % 2, :, :],
                                       in_=xin[:, :, i + 1, :].transpose([1, 0, 2]))
                P.add("sync", ldx2, dma=True, writes=[f"xt{(i + 1) % 2}"])

            xres = f"xt{s}"
            xv = lambda dc, s=s: xt[:, s, dc, :]
            layernorm(xv, xres)
            dump(i, "dbg_normed1", normed, "normed")
            router()
            dump(i, "dbg_gbc", gbc, "gbc")
            for dc in range(DC):
                def cp(e, dc=dc, s=s):
                    return e.tensor_copy(acc[:, dc, :], xt[:, s, dc, :])
                P.add("vector", cp, reads=[xres], writes=["acc"])
            if i == 0:
                def after_e(e_):
                    if e_ + 2 < NE:
                        cast_expert(e_ + 2, gate="acc")
                    if e_ == NE - 2:
                        cast_qkv()
                ffn(i, on_expert_done=after_e)
            else:
                ffn(i)
            dump(i, "dbg_h", sb["hbuf"], "hbuf")
            dump(i, "dbg_x2", acc, "acc")
            av = lambda dc: acc[:, dc, :]
            layernorm(av, "acc")
            def stx2(e, i=i):
                return e.dma_start(out=x2d[:, :, i, :].transpose([1, 0, 2]), in_=acc[:])
            P.add("gpsimd", stx2, dma=True, reads=["acc"], writes=["x2dd"])
            qkv(i)
            def stq(e, i=i):
                return e.dma_start(out=qdr[:, :, i, :].transpose([1, 0, 2]), in_=qbf[:])
            P.add("gpsimd", stq, dma=True, reads=["qbf"], writes=["qdrd"])
            dump(i, "dbg_q", qbf, "qbf")
            dump(i, "dbg_k", kst, "kst")
            dump(i, "dbg_v", vst, "vst")

        def build_block_b(i):
            def ldx2b(e, i=i):
                return e.dma_start(out=acc[:], in_=x2d[:, :, i, :].transpose([1, 0, 2]))
            P.add("sync", ldx2b, dma=True, reads=["x2dd"], writes=["acc"])
            def ldq(e, i=i):
                return e.dma_start(out=qbf[:], in_=qdr[:, :, i, :].transpose([1, 0, 2]))
            P.add("sync", ldq, dma=True, reads=["qdrd"], writes=["qbf"])
            attention(i)
            dump(i, "dbg_x3", acc, "acc")
            av = lambda dc: acc[:, dc, :]
            layernorm(av, "acc")
            router()
            ffn(i)
            def sto(e, i=i):
                dst = outap[:, :, i, :].transpose([1, 0, 2])
                return e.dma_start(out=dst, in_=acc[:])
            P.add("gpsimd", sto, dma=True, reads=["acc"], writes=["outd"])

        prologue()
        for i in range(NB):
            build_block_a(i)
        kscr_pass()
        if DEBUG_DUMPS:
            P.add("gpsimd", lambda e: e.dma_start(out=dbg["dbg_kdr"], in_=kdr),
                  dma=True, reads=["kdrd"], writes=["dbg_kdr"])
            def _dkscr(e):
                return e.dma_start(out=dbg["dbg_kscr"], in_=kscr[0:2, 0:4])
            P.add("gpsimd", _dkscr, dma=True, reads=["kscrd"], writes=["dbg_kscr"])
        for i in range(NB):
            build_block_b(i)

        P.resolve()

        import contextlib
        with contextlib.ExitStack() as es:
            sems = {}
            for ki, key in enumerate(P.sem_keys):
                sems[key] = es.enter_context(nc.semaphore(f"sm{ki}"))
            block = es.enter_context(nc.Block())

            @block.sync
            def _(e):
                P.emit("sync", e, sems)

            @block.scalar
            def _(e):
                P.emit("scalar", e, sems)

            @block.vector
            def _(e):
                P.emit("vector", e, sems)

            @block.tensor
            def _(e):
                P.emit("tensor", e, sems)

            @block.gpsimd
            def _(e):
                P.emit("gpsimd", e, sems)
                e.wait_ge(sems[("r", "outd")], P.dma_final["outd"])

    return nc, sbuf_left, len(P.ops)


_NC_CACHE = None


def kernel(L0_attn_x, L0_attn_qkv_weights, L0_ffn_router_W, L0_ffn_router_b,
           L0_ffn_W1, L0_ffn_b1, L0_ffn_W2, L0_ffn_b2):
    global _NC_CACHE
    x = np.asarray(L0_attn_x, dtype=np.float32)
    qkvw = np.asarray(L0_attn_qkv_weights, dtype=np.float32)
    rW = np.asarray(L0_ffn_router_W, dtype=np.float32)
    rb = np.asarray(L0_ffn_router_b, dtype=np.float32)
    W1 = np.asarray(L0_ffn_W1, dtype=np.float32)
    b1 = np.asarray(L0_ffn_b1, dtype=np.float32)
    W2 = np.asarray(L0_ffn_W2, dtype=np.float32)
    b2 = np.asarray(L0_ffn_b2, dtype=np.float32)

    if _NC_CACHE is None:
        _NC_CACHE = build_nc()
    nc, _, _ = _NC_CACHE

    qkvw_l = np.ascontiguousarray(qkvw.reshape(DC, 128, 3 * D))
    rwT_l = np.ascontiguousarray(rW.reshape(DC, 128, NE).transpose(1, 0, 2)).reshape(128, DC * NE)
    rb_l = np.ascontiguousarray(np.broadcast_to(rb[None, :], (128, NE)))
    b1T_l = np.ascontiguousarray(b1.reshape(NE, FCN, 128).transpose(2, 0, 1)).reshape(128, NE * FCN)
    b2T_l = np.ascontiguousarray(b2.reshape(NE, DC, 128).transpose(2, 0, 1)).reshape(128, NE * DC)
    w1_l = np.ascontiguousarray(W1.reshape(NE, DC, 128, DFF))
    w2_l = np.ascontiguousarray(W2.reshape(NE, FCN, 128, D))

    in_maps = []
    for b in range(8):
        x_fm = np.ascontiguousarray(x[b].T).reshape(DC, 128, NB, T)
        in_maps.append({
            "x_fm": x_fm, "qkvw": qkvw_l, "rwT": rwT_l, "rb_bc": rb_l,
            "b1T": b1T_l, "b2T": b2T_l, "w1": w1_l, "w2": w2_l,
        })
    res = run_bass_kernel_spmd(nc, in_maps, core_ids=list(range(8)))
    out = np.empty((8, S, D), np.float32)
    for b in range(8):
        o = res.results[b]["out_fm"]
        out[b] = o.transpose(2, 3, 0, 1).reshape(S, D)
    return out
